# version 07
# baseline (speedup 1.0000x reference)
import numpy as np
import ml_dtypes
import concourse.bass as bass
import concourse.mybir as mybir
from concourse.bass_utils import run_bass_kernel_spmd
from contextlib import ExitStack

DT = mybir.dt
F32 = DT.float32
BF16 = DT.bfloat16
AF = mybir.ActivationFunctionType
ALU = mybir.AluOpType
AX = mybir.AxisListType

L, D, DIN, NH, HD, DS, Q, H = 4096, 1024, 2048, 8, 256, 16, 128, 4096
T1 = 256
NT1 = L // T1
T2 = 512
NT2 = L // T2
MUSE = 6184

ENGS = ("sync", "tensor", "vector", "scalar", "gpsimd")
E2S = {"tensor": "PE", "vector": "DVE", "scalar": "ACT", "gpsimd": "POOL"}


class Builder:
    def __init__(self, nc, ctx):
        self.nc = nc
        self.q = {e: [] for e in ENGS}
        self.sems = {}
        self.cnt = {}
        self.pending = {e: None for e in ENGS}
        for s in ("LDP", "LD0", "LD1", "LDG", "LDT", "LT0", "LT1", "ST", "ST2", "PE", "DVE", "ACT", "POOL"):
            self.sems[s] = ctx.enter_context(nc.semaphore(s))
            self.cnt[s] = 0
        self.regions = []
        self.block = ctx.enter_context(nc.Block())

    def op(self, eng, fn, inc=None):
        cell = [None]
        sems = self.sems

        def run(e, fn=fn, cell=cell):
            ins = fn(e)
            if cell[0] is not None:
                ins.then_inc(sems[cell[0][0]], cell[0][1])
        self.q[eng].append(run)
        if inc is not None:
            self.cnt[inc] += 1
            cell[0] = (inc, 1)
            self.pending[eng] = None
            return self.cnt[inc]
        self.pending[eng] = cell
        return None

    def mark(self, eng):
        sem = E2S[eng]
        cell = self.pending[eng]
        if cell is None:
            return self.cnt[sem]
        self.cnt[sem] += 1
        cell[0] = (sem, 1)
        self.pending[eng] = None
        return self.cnt[sem]

    def dma(self, eng, out, in_, sem):
        self.cnt[sem] += 16
        val = self.cnt[sem]
        s = self.sems[sem]

        def run(e, out=out, in_=in_, s=s):
            e.dma_start(out=out, in_=in_).then_inc(s, 16)
        self.q[eng].append(run)
        self.pending[eng] = None
        return val

    def wait(self, eng, sem, val):
        if not val:
            return
        s = self.sems[sem]

        def run(e, s=s, val=val):
            e.wait_ge(s, val)
        self.q[eng].append(run)

    def use(self, eng, tag, lo=0, hi=1 << 30):
        keep = []
        own = E2S.get(eng)
        for ent in self.regions:
            t, a, b_, sem, val = ent
            if t == tag and a < hi and lo < b_:
                if sem in ("ST", "ST2"):
                    self.wait(eng, sem, self.cnt[sem])
                else:
                    self.wait(eng, sem, val)
            else:
                keep.append(ent)
        self.regions = keep

    def rel(self, tag, sem, val, lo=0, hi=1 << 30):
        self.regions = [ent for ent in self.regions
                        if not (ent[0] == tag and ent[3] == sem
                                and lo <= ent[1] and ent[2] <= hi)]
        self.regions.append((tag, lo, hi, sem, val))

    def barrier(self):
        for e in E2S:
            self.mark(e)
        vals = {s: self.cnt[s] for s in ("PE", "DVE", "ACT", "POOL", "LDP", "LD0", "LD1", "LDG", "LDT", "LT0", "LT1", "ST", "ST2")}
        for e in ENGS:
            for s, v in vals.items():
                if v > 0:
                    self.wait(e, s, v)
        self.regions = []

    def flush(self):
        for eng in ENGS:
            ops = self.q[eng]
            self.q[eng] = []
            if not ops:
                continue
            getattr(self.block, eng)(lambda e, ops=ops: [f(e) for f in ops])


def build_program(debug=False, phases="012ab3", nt1=NT1, nt2=NT2, p1_stage=6, races=False):
    nc = bass.Bass("TRN2", target_bir_lowering=False, detect_race_conditions=races)
    ctx = ExitStack()

    def din(name, shape, dt=F32):
        return nc.dram_tensor(name, list(shape), dt, kind="ExternalInput").ap()

    uT = din("uT", [D, L])
    winT = din("winT", [D, MUSE])
    wgT = din("wgT", [DIN, DIN])
    woT = din("woT", [2 * DIN, D])
    wfcT = din("wfcT", [D, H])
    wprT = din("wprT", [H, D])
    cwx = din("cwx", [128, 64])
    cwb = din("cwb", [16, 4])
    cwc = din("cwc", [16, 4])
    cbx = din("cbx", [128, 16])
    cbb = din("cbb", [16, 1])
    cbc = din("cbc", [16, 1])
    dtb = din("dtb", [NH, 1])
    alog = din("alog", [NH, 1])
    dpar = din("dpar", [NH, 1])
    lng = din("lng", [128, 16])
    lnbt = din("lnbt", [128, 16])
    bgat = din("bgat", [128, 16])
    flnw = din("flnw", [128, 32])
    flnb = din("flnb", [128, 32])
    cons = din("cons", [128, 768])
    consb = din("consb", [128, 768], BF16)

    out_d = nc.dram_tensor("out", [D, L], F32, kind="ExternalOutput").ap()
    skind = "ExternalOutput" if debug else "Internal"
    wg_bf = nc.dram_tensor("wg_bf", [DIN, DIN], BF16, kind="Internal").ap()
    wo_bf = nc.dram_tensor("wo_bf", [2 * DIN, D], BF16, kind="Internal").ap()
    wfcq = nc.dram_tensor("wfcq", [D, H], BF16, kind=skind).ap()
    wprq = nc.dram_tensor("wprq", [H, D], BF16, kind="Internal").ap()
    yln_sp = nc.dram_tensor("yln_sp", [DIN, L], BF16, kind=skind).ap()
    mlp_sp = nc.dram_tensor("mlp_sp", [DIN, L], BF16, kind=skind).ap()
    ygat_sp = nc.dram_tensor("ygat_sp", [DIN, L], BF16, kind="Internal").ap()
    x1_sp = nc.dram_tensor("x1_sp", [D, L], F32, kind=skind).ap()
    st_sp = nc.dram_tensor("st_sp", [2, L], F32, kind=skind).ap()
    y_dbg = nc.dram_tensor("y_dbg", [DIN, L], BF16, kind=skind).ap() if debug else None
    xbf_dbg = nc.dram_tensor("xbf_dbg", [DIN, L], BF16, kind=skind).ap() if debug else None
    xtm_dbg = nc.dram_tensor("xtm_dbg", [L, DIN], BF16, kind=skind).ap() if debug else None
    ytm_dbg = nc.dram_tensor("ytm_dbg", [L, DIN], BF16, kind=skind).ap() if debug else None
    px_dbg = nc.dram_tensor("px_dbg", [128, 256], BF16, kind=skind).ap() if debug else None
    y1_dbg = nc.dram_tensor("y1_dbg", [128, 256], BF16, kind=skind).ap() if debug else None
    yp_dbg = nc.dram_tensor("yp_dbg", [128, 256], F32, kind=skind).ap() if debug else None
    yf1_dbg = nc.dram_tensor("yf1_dbg", [128, 256], BF16, kind=skind).ap() if debug else None
    yf2_dbg = nc.dram_tensor("yf2_dbg", [128, 256], BF16, kind=skind).ap() if debug else None
    yf3_dbg = nc.dram_tensor("yf3_dbg", [128, 256], BF16, kind=skind).ap() if debug else None
    yf4_dbg = nc.dram_tensor("yf4_dbg", [128, 256], BF16, kind=skind).ap() if debug else None
    yf5_dbg = nc.dram_tensor("yf5_dbg", [128, 256], BF16, kind=skind).ap() if debug else None

    b = Builder(nc, ctx)

    SB = lambda n, shp, dt=F32: nc.alloc_sbuf_tensor(n, list(shp), dt).ap()
    cons_sb = SB("cons_sb", [128, 768])
    consb_sb = SB("consb_sb", [128, 768], BF16)
    cbrow_sb = SB("cbrow_sb", [1, D], BF16)
    cwx_sb = SB("cwx_sb", [128, 64]); cwb_sb = SB("cwb_sb", [16, 4]); cwc_sb = SB("cwc_sb", [16, 4])
    cbx_sb = SB("cbx_sb", [128, 16]); cbb_sb = SB("cbb_sb", [16, 1]); cbc_sb = SB("cbc_sb", [16, 1])
    dtb_sb = SB("dtb_sb", [NH, 1]); alog_sb = SB("alog_sb", [NH, 1]); dpar_sb = SB("dpar_sb", [NH, 1])
    lng_sb = SB("lng_sb", [128, 16]); lnbt_sb = SB("lnbt_sb", [128, 16]); bgat_sb = SB("bgat_sb", [128, 16])
    flnw_sb = SB("flnw_sb", [128, 32]); flnb_sb = SB("flnb_sb", [128, 32])
    flnbh_sb = SB("flnbh_sb", [128, 32], BF16)
    acol_sb = SB("acol_sb", [NH, 1])
    dtm_sb = SB("dtm_sb", [128, NH])
    diag_sb = SB("diag_sb", [NH, NH])
    gcol_sb = SB("gcol_sb", [128, 4])
    grow_sb = SB("grow_sb", [1, 2])
    eps_sb = SB("eps_sb", [128, 1])

    IDE = cons_sb[:, 0:128]
    ONES_COL = cons_sb[:, 128:129]
    ONES_ROW1 = cons_sb[0:1, 128:256]
    ONES8x16 = cons_sb[0:NH, 128:144]
    ONES8x128 = cons_sb[0:NH, 128:256]
    IDE_BF = consb_sb[:, 0:128]
    TRIU = consb_sb[:, 128:256]
    ONES_COL_BF = consb_sb[:, 255:256]
    ONES_ROW_BF = consb_sb[0:1, 256:768]

    PS = lambda n, shp, dt=F32: nc.alloc_psum_tensor(n, list(shp), dt).ap()
    pA = PS("pA", [128, 512])
    pB = PS("pB", [128, 512])
    pC = PS("pC", [128, 512])
    pE = PS("pE", [128, 512])
    pS = PS("pS", [128, 512])
    pX1 = PS("pX1", [128, 1024], BF16)
    pX2 = PS("pX2", [128, 1024], BF16)
    pY = PS("pY", [128, 512])

    v = None
    for dst, src in [(cons_sb, cons), (consb_sb, consb), (cwx_sb, cwx), (cwb_sb, cwb),
                     (cwc_sb, cwc), (cbx_sb, cbx), (cbb_sb, cbb), (cbc_sb, cbc),
                     (dtb_sb, dtb), (alog_sb, alog), (dpar_sb, dpar), (lng_sb, lng),
                     (lnbt_sb, lnbt), (bgat_sb, bgat), (flnw_sb, flnw), (flnb_sb, flnb)]:
        v = b.dma("sync", dst, src, "LDP")
    b.wait("scalar", "LDP", v)
    b.op("scalar", lambda e: e.activation(out=acol_sb, in_=alog_sb, func=AF.Exp))
    va = b.op("scalar", lambda e: e.copy(out=flnbh_sb, in_=flnb_sb), inc="ACT")
    b.wait("vector", "ACT", va)
    b.op("vector", lambda e: e.tensor_scalar_mul(out=eps_sb, in0=cons_sb[:, 128:129], scalar1=1e-5))
    b.op("vector", lambda e: e.tensor_scalar_mul(out=acol_sb, in0=acol_sb, scalar1=-1.0))
    b.op("vector", lambda e: e.tensor_scalar_mul(out=flnbh_sb, in0=flnbh_sb, scalar1=0.5))
    vd = b.op("vector", lambda e: e.tensor_scalar_mul(
        out=diag_sb, in0=cons_sb[0:NH, 0:NH], scalar1=dpar_sb), inc="DVE")
    b.wait("tensor", "DVE", vd)
    vm = b.op("tensor", lambda e: e.matmul(pE[:, 0:NH], ONES8x128, diag_sb,
                                           start=True, stop=True), inc="PE")
    b.wait("vector", "PE", vm)
    vt = b.op("vector", lambda e: e.tensor_copy(out=dtm_sb, in_=pE[:, 0:NH]), inc="DVE")
    b.rel("pE", "DVE", vt, 0, 512)

    pws = ExitStack()
    win_sb = pws.enter_context(nc.sbuf_tensor("win_sb", [128, 8 * MUSE], BF16)).ap()
    with ExitStack() as p0s:
        A0 = lambda n, shp, dt=F32: p0s.enter_context(nc.sbuf_tensor(n, list(shp), dt)).ap()
        land_t = A0("p0_land", [128, 2 * 4096])
        bf_t = A0("p0_bf", [128, 2 * 4096], BF16)
        m_t = A0("p0_m", [128, 2 * 4096], BF16)
        g_t = A0("p0_g", [128, 48])

        LQ = ("sync", "gpsimd")
        LS = ("LD0", "LDG")

        def cast_weight(src, dst, rows, cols, tag):
            for k in range(rows // 128):
                par = k % 2
                land = land_t[:, par * 4096: par * 4096 + cols]
                bft = bf_t[:, par * 4096: par * 4096 + cols]
                b.use(LQ[par], f"p0L{par}")
                vl = b.dma(LQ[par], land, src[k * 128:(k + 1) * 128, :], LS[par])
                b.use("vector", f"p0B{par}")
                b.wait("vector", LS[par], vl)
                vc = b.op("vector", lambda e, land=land, bft=bft: e.tensor_copy(out=bft, in_=land), inc="DVE")
                b.rel(f"p0L{par}", "DVE", vc)
                b.wait("scalar", "DVE", vc)
                vs = b.dma("scalar", dst[k * 128:(k + 1) * 128, :], bft, "ST2")
                b.rel(f"p0B{par}", "ST2", vs)

        for k in range(8):
            for c0w, cw in ((0, 3092), (3092, MUSE - 3092)):
                par = (k * 2 + (0 if c0w == 0 else 1)) % 2
                land = land_t[:, par * 4096: par * 4096 + cw]
                b.use(LQ[par], f"p0L{par}")
                vl = b.dma(LQ[par], land, winT[k * 128:(k + 1) * 128, c0w:c0w + cw], LS[par])
                b.wait("vector", LS[par], vl)
                vc = b.op("vector", lambda e, land=land, k=k, c0w=c0w, cw=cw: e.tensor_copy(
                    out=win_sb[:, k * MUSE + c0w:k * MUSE + c0w + cw], in_=land), inc="DVE")
                b.rel(f"p0L{par}", "DVE", vc)
        cast_weight(wgT, wg_bf, DIN, DIN, "wg")
        cast_weight(woT, wo_bf, 2 * DIN, D, "wo")

        def quantize_weight(src, dst, rows, cols, gi, tag, with_cb=False):
            nk = rows // 128
            for k in range(nk):
                par = k % 2
                land = land_t[:, par * 4096: par * 4096 + cols]
                b.use(LQ[par], f"p0L{par}")
                vl = b.dma(LQ[par], land, src[k * 128:(k + 1) * 128, :], LS[par])
                b.wait("scalar", LS[par], vl)
                b.use("scalar", f"p0M{par}")
                mt = m_t[:, par * 4096: par * 4096 + cols]
                vr = b.op("scalar", lambda e, land=land, k=k, mt=mt: e.activation(
                    out=mt, in_=land, func=AF.Abs, accum_out=g_t[:, k:k + 1]), inc="ACT")
                b.rel(f"p0L{par}", "ACT", vr)
                b.rel(f"p0M{par}", "ACT", vr)
            b.wait("vector", "ACT", vr)
            vr2 = b.op("vector", lambda e, nk=nk: e.reduce_sum(
                out=g_t[:, 40:41], in_=g_t[:, 0:nk], axis=AX.X), inc="DVE")
            b.wait("tensor", "DVE", vr2)
            b.use("tensor", "pS", 0, 2)
            vm0 = b.op("tensor", lambda e: e.matmul(pS[0:1, 0:1], g_t[:, 40:41], ONES_COL,
                                                    start=True, stop=True), inc="PE")
            b.wait("vector", "PE", vm0)
            sc1 = 0.5 / (rows * cols)
            vc0 = b.op("vector", lambda e, sc1=sc1: e.tensor_scalar(
                out=grow_sb[0:1, 0:1], in0=pS[0:1, 0:1], scalar1=sc1, scalar2=0.5e-6,
                op0=ALU.mult, op1=ALU.add), inc="DVE")
            b.rel("pS", "DVE", vc0, 0, 2)
            b.wait("tensor", "DVE", vc0)
            b.use("tensor", "pS", 0, 2)
            vbc = b.op("tensor", lambda e: e.matmul(pS[:, 1:2], ONES_ROW1, grow_sb[0:1, 0:1],
                                                    start=True, stop=True), inc="PE")
            b.wait("vector", "PE", vbc)
            b.op("vector", lambda e, gi=gi: e.tensor_copy(out=gcol_sb[:, 2 * gi:2 * gi + 1], in_=pS[:, 1:2]))
            vng = b.op("vector", lambda e, gi=gi: e.tensor_scalar_mul(
                out=gcol_sb[:, 2 * gi + 1:2 * gi + 2], in0=pS[:, 1:2], scalar1=-1.0), inc="DVE")
            b.rel("pS", "DVE", vng, 0, 2)
            b.wait("scalar", "DVE", vng)
            vcb = None
            for k in range(nk):
                par = k % 2
                land = land_t[:, par * 4096: par * 4096 + cols]
                mt = m_t[:, par * 4096: par * 4096 + cols]
                bft = bf_t[:, par * 4096: par * 4096 + cols]
                b.use(LQ[par], f"p0L{par}")
                vl = b.dma(LQ[par], land, src[k * 128:(k + 1) * 128, :], LS[par])
                b.wait("scalar", LS[par], vl)
                b.use("scalar", f"p0M{par}")
                b.op("scalar", lambda e, land=land, mt=mt, gi=gi: e.activation(
                    out=mt, in_=land, func=AF.Sign, bias=gcol_sb[:, 2 * gi + 1:2 * gi + 2]))
                b.use("scalar", f"p0B{par}")
                vs2 = b.op("scalar", lambda e, land=land, bft=bft, gi=gi: e.activation(
                    out=bft, in_=land, func=AF.Sign, bias=gcol_sb[:, 2 * gi:2 * gi + 1]), inc="ACT")
                b.rel(f"p0L{par}", "ACT", vs2)
                b.wait("vector", "ACT", vs2)
                vq = b.op("vector", lambda e, mt=mt, bft=bft: e.tensor_add(out=bft, in0=bft, in1=mt), inc="DVE")
                b.rel(f"p0M{par}", "DVE", vq)
                if with_cb:
                    b.wait("tensor", "DVE", vq)
                    if k == 0:
                        b.use("tensor", "pY", 0, 512)
                        b.use("tensor", "pE", 0, 512)
                    for j2, bank in enumerate((pY, pE)):
                        b.op("tensor", lambda e, bft=bft, k=k, j2=j2, nk=nk, bank=bank: e.matmul(
                            bank[0:1, 0:512],
                            flnbh_sb[:, k:k + 1], bft[:, j2 * 512:(j2 + 1) * 512],
                            start=(k == 0), stop=(k == nk - 1)))
                    vcb = b.mark("tensor")
                    b.rel(f"p0B{par}", "PE", vcb)
                b.wait("gpsimd", "DVE", vq)
                vst = b.dma("gpsimd", dst[k * 128:(k + 1) * 128, :], bft, "ST")
                b.rel(f"p0B{par}", "ST", vst)
            if with_cb:
                b.wait("vector", "PE", vcb)
                b.op("vector", lambda e: e.tensor_copy(out=cbrow_sb[0:1, 0:512], in_=pY[0:1, 0:512]))
                vce = b.op("vector", lambda e: e.tensor_copy(out=cbrow_sb[0:1, 512:1024], in_=pE[0:1, 0:512]), inc="DVE")
                b.rel("pY", "DVE", vce, 0, 512)
                b.rel("pE", "DVE", vce, 0, 512)

        quantize_weight(wfcT, wfcq, D, H, 0, "fc")
        quantize_weight(wprT, wprq, H, D, 1, "pr", with_cb=True)
        b.barrier()
        b.flush()

    if "1" not in phases:
        b.wait("sync", "ST", b.cnt["ST"])
        b.flush()
        ctx.close()
        return nc
    with ExitStack() as p1s:
        A = lambda n, shp, dt=F32: p1s.enter_context(nc.sbuf_tensor(n, list(shp), dt)).ap()
        u_sb = A("u_sb", [128, 2 * 8 * T1])
        u2_sb = A("u2_sb", [128, 8 * T1], BF16)
        usq_sb = A("usq_sb", [128, 2 * T1], BF16)
        cvx_sb = A("cvx_sb", [128, 16 * (T1 + 3)], BF16)
        cvb_sb = A("cvb_sb", [16, T1 + 3], BF16)
        cvc_sb = A("cvc_sb", [16, T1 + 3], BF16)
        cacc_sb = A("cacc_sb", [128, 4 * T1])
        xbf_sb = A("xbf_sb", [128, 16 * T1], BF16)
        bbf_sb = A("bbf_sb", [16, T1], BF16)
        cbf_sb = A("cbf_sb", [16, T1], BF16)
        xtm_sb = A("xtm_sb", [128, 2 * 2048], BF16)
        btm_sb = A("btm_sb", [128, 2 * 16], BF16)
        gm_sb = A("gm_sb", [128, 2 * 128], BF16)
        wp_sb = A("wp_sb", [128, 2 * 8 * 128], BF16)
        xq_sb = A("xq_sb", [128, 4 * 256], BF16)
        y1_sb = A("y1_sb", [128, 2 * 256], BF16)
        ytm_sb = A("ytm_sb", [128, 8 * 256], BF16)
        yfm_sb = A("yfm_sb", [128, 16 * T1], BF16)
        ysq_sb = A("ysq_sb", [128, 2 * T1], BF16)
        mlp_sb = A("mlp_sb", [128, 4 * T1], BF16)
        z0s_sb = A("z0s_sb", [128, 2 * T1], BF16)
        yln_sb = A("yln_sb", [128, 4 * T1], BF16)
        dt_sb = A("dt_sb", [NH, T1])
        dte_sb = A("dte_sb", [NH, T1])
        ddt_sb = A("ddt_sb", [NH, T1])
        zero8_sb = A("zero8_sb", [NH, Q])
        s_sb = A("s_sb", [NH, Q])
        r_sb = A("r_sb", [NH, Q])
        exps_sb = A("exps_sb", [NH, Q])
        ens_sb = A("ens_sb", [NH, Q])
        er_sb = A("er_sb", [NH, Q])
        nsdt_sb = A("nsdt_sb", [NH, Q])
        qf_sb = A("qf_sb", [NH, Q])
        ecol8_sb = A("ecol8_sb", [NH, 1])
        diage_sb = A("diage_sb", [NH, NH])
        tm_sb = A("tm_sb", [128, 24])
        ecol16_sb = A("ecol16_sb", [16, NH])
        h_sb = A("h_sb", [16, 2048])
        hbf_sb = A("hbf_sb", [16, 2048], BF16)
        row_a = A("row_a", [1, T1])
        row_b = A("row_b", [1, T1])
        row_c = A("row_c", [1, T1])
        rs0b_sb = A("rs0b_sb", [128, T1], BF16)
        ryb_sb = A("ryb_sb", [128, T1], BF16)
        mrb_sb = A("mrb_sb", [128, T1], BF16)

        b.op("vector", lambda e: e.memset(zero8_sb, 0.0))
        b.op("vector", lambda e: e.memset(h_sb, 0.0))
        b.op("vector", lambda e: e.memset(hbf_sb, 0.0))
        b.op("vector", lambda e: e.memset(cvx_sb, 0.0))
        b.op("vector", lambda e: e.memset(cvb_sb, 0.0))
        vini = b.op("vector", lambda e: e.memset(cvc_sb, 0.0), inc="DVE")
        b.wait("scalar", "DVE", vini)
        b.wait("tensor", "DVE", vini)
        b.wait("gpsimd", "DVE", vini)

        mplan = []
        for g in range(16):
            mplan.append(("z0", g, g * 128, 128))
            mplan.append(("x0", g, 2048 + g * 128, 128))
        for g in range(16):
            mplan.append(("X", g, 4096 + g * 128, 128))
        mplan += [("B", 0, 6144, 16), ("C", 0, 6160, 16), ("dt", 0, 6176, 8)]

        uval = [None] * (NT1 + 2)
        for i in range(min(2, NT1)):
            par = i % 2
            for g in range(8):
                uval[i] = b.dma("sync", u_sb[:, (par * 8 + g) * T1:(par * 8 + g + 1) * T1],
                                uT[g * 128:(g + 1) * 128, i * T1:(i + 1) * T1], f"LD{par}")

        for it in range(nt1):
            par = it % 2
            uofs = par * 8 * T1
            t0 = it * T1

            b.wait("tensor", f"LD{par}", uval[it])
            b.wait("scalar", f"LD{par}", uval[it])
            b.use("tensor", "pS", 0, 2 * T1)
            b.use("tensor", "pY", 0, T1)
            pend_u = None
            for g in range(8):
                usrc = u_sb[:, uofs + g * T1: uofs + (g + 1) * T1]
                b.op("tensor", lambda e, usrc=usrc, g=g: e.matmul(
                    pS[0:1, 0:T1], ONES_COL, usrc, start=(g == 0), stop=(g == 7)))
                sq = g % 2
                b.use("scalar", f"usq{sq}")
                vsq = b.op("scalar", lambda e, usrc=usrc, sq=sq: e.activation(
                    out=usq_sb[:, sq * T1:(sq + 1) * T1], in_=usrc, func=AF.Square), inc="ACT")
                for pg, pvsq in ([pend_u] if pend_u else []) + ([(g, vsq)] if g == 7 else []):
                    psq = pg % 2
                    b.wait("tensor", "ACT", pvsq)
                    vqm = b.op("tensor", lambda e, psq=psq, pg=pg: e.matmul(
                        pY[0:1, 0:T1], ONES_COL_BF, usq_sb[:, psq * T1:(psq + 1) * T1],
                        start=(pg == 0), stop=(pg == 7)), inc="PE")
                    b.rel(f"usq{psq}", "PE", vqm)
                pend_u = (g, vsq)
            vstat = b.mark("tensor")
            b.wait("vector", "PE", vstat)
            b.op("vector", lambda e: e.tensor_scalar_mul(out=row_a, in0=pS[0:1, 0:T1], scalar1=1.0 / D))
            vrb = b.op("vector", lambda e: e.tensor_scalar_mul(out=row_b, in0=pY[0:1, 0:T1], scalar1=1.0 / D), inc="DVE")
            b.wait("vector", "DVE", vrb)
            vrc = b.op("vector", lambda e: e.tensor_mul(out=row_c, in0=row_a, in1=row_a), inc="DVE")
            b.wait("vector", "DVE", vrc)
            vvar = b.op("vector", lambda e: e.tensor_sub(out=row_b, in0=row_b, in1=row_c), inc="DVE")
            b.rel("pS", "DVE", vvar, 0, 2 * T1)
            b.rel("pY", "DVE", vvar, 0, T1)
            b.wait("scalar", "DVE", vvar)
            vln = b.op("scalar", lambda e: e.activation(out=row_b, in_=row_b, func=AF.Ln, bias=eps_sb[0:1, 0:1]), inc="ACT")
            b.wait("scalar", "ACT", vln)
            vrs = b.op("scalar", lambda e: e.activation(out=row_b, in_=row_b, func=AF.Exp, scale=-0.5), inc="ACT")
            b.wait("tensor", "ACT", vrs)
            b.use("tensor", "pC", 0, 2 * T1)
            b.op("tensor", lambda e: e.matmul(pC[:, 0:T1], ONES_ROW1, row_a, start=True, stop=True))
            vbc = b.op("tensor", lambda e: e.matmul(pC[:, T1:2 * T1], ONES_ROW1, row_b,
                                                    start=True, stop=True), inc="PE")
            b.wait("vector", "PE", vbc)
            b.op("vector", lambda e: e.tensor_copy(out=rs0b_sb, in_=pC[:, T1:2 * T1]))
            b.use("vector", "u2k")
            for g in range(8):
                usrc = u_sb[:, uofs + g * T1: uofs + (g + 1) * T1]
                dst = u2_sb[:, g * T1:(g + 1) * T1]
                b.op("vector", lambda e, usrc=usrc, dst=dst: e.tensor_sub(out=dst, in0=usrc, in1=pC[:, 0:T1]))
            vfu = b.mark("vector")
            b.wait("vector", "DVE", vfu)
            for g in range(8):
                dst = u2_sb[:, g * T1:(g + 1) * T1]
                b.op("vector", lambda e, dst=dst: e.tensor_mul(out=dst, in0=dst, in1=rs0b_sb))
            vu2 = b.mark("vector")
            b.rel("pC", "DVE", vu2, 0, 2 * T1)
            b.rel(f"ub{par}", "DVE", vu2)
            b.wait("tensor", "DVE", vu2)

            if it + 2 < nt1:
                b.use("sync", f"ub{par}")
                for g in range(8):
                    uval[it + 2] = b.dma("sync", u_sb[:, (par * 8 + g) * T1:(par * 8 + g + 1) * T1],
                                         uT[g * 128:(g + 1) * 128, (it + 2) * T1:(it + 3) * T1], f"LD{par}")
            if p1_stage < 2:
                continue

            mmr = [pA, pB]
            z0info = {}
            dt_pdst = dt_rb = dt_vmm = None
            for mi, (kind, gch, coff, mw) in enumerate(mplan):
                rb = mi % 2
                mmtag = ("pA", "pB")[rb]
                pdst = mmr[rb][0:mw, 0:T1]
                b.use("tensor", mmtag, 0, 512)
                for k in range(8):
                    b.op("tensor", lambda e, pdst=pdst, k=k, coff=coff, mw=mw: e.matmul(
                        pdst, win_sb[:, k * MUSE + coff:k * MUSE + coff + mw],
                        u2_sb[:, k * T1:(k + 1) * T1], start=(k == 0), stop=(k == 7)))
                vmm = b.mark("tensor")
                if kind == "z0":
                    zp = gch % 2
                    b.use("scalar", f"z0s{zp}")
                    b.wait("scalar", "PE", vmm)
                    vz = b.op("scalar", lambda e, pdst=pdst, zp=zp: e.activation(
                        out=z0s_sb[:, zp * T1:(zp + 1) * T1], in_=pdst, func=AF.Silu), inc="ACT")
                    b.rel(mmtag, "ACT", vz, 0, 512)
                    z0info[gch] = (zp, vz)
                elif kind == "x0":
                    zp, vz = z0info[gch]
                    mp = gch % 4
                    b.use("vector", f"mlp{mp}")
                    b.wait("vector", "PE", vmm)
                    b.wait("vector", "ACT", vz)
                    vml = b.op("vector", lambda e, pdst=pdst, zp=zp, mp=mp: e.tensor_mul(
                        out=mlp_sb[:, mp * T1:(mp + 1) * T1], in0=pdst,
                        in1=z0s_sb[:, zp * T1:(zp + 1) * T1]), inc="DVE")
                    b.rel(mmtag, "DVE", vml, 0, 512)
                    b.rel(f"z0s{zp}", "DVE", vml)
                    b.wait("gpsimd", "DVE", vml)
                    vsp = b.dma("gpsimd", mlp_sp[gch * 128:(gch + 1) * 128, t0:t0 + T1],
                                mlp_sb[:, mp * T1:(mp + 1) * T1], "ST")
                    b.rel(f"mlp{mp}", "ST", vsp)
                elif kind in ("X", "B", "C"):
                    if kind == "X":
                        dstb = cvx_sb[:, gch * (T1 + 3) + 3:(gch + 1) * (T1 + 3)]
                        key = f"cvx{gch}"
                    elif kind == "B":
                        dstb = cvb_sb[:, 3:T1 + 3]; key = "cvB"
                    else:
                        dstb = cvc_sb[:, 3:T1 + 3]; key = "cvC"
                    b.use("scalar", key)
                    b.wait("scalar", "PE", vmm)
                    vx = b.op("scalar", lambda e, pdst=pdst, dstb=dstb: e.copy(out=dstb, in_=pdst), inc="ACT")
                    b.rel(mmtag, "ACT", vx, 0, 512)
                    b.rel(f"evac_{key}", "ACT", vx)
                else:
                    dt_pdst, dt_rb, dt_vmm = pdst, rb, vmm
            b.rel("u2k", "PE", b.cnt["PE"])
            if p1_stage < 3:
                continue

            silu_done = {}
            for grp in ([0, 1, 2, 3], [4, 5, 6, 7], [8, 9, 10, 11], [12, 13, 14, 15], [16, 17]):
                info = {}
                for c in grp:
                    if c < 16:
                        buf = cvx_sb[:, c * (T1 + 3):(c + 1) * (T1 + 3)]
                        taps = [cwx_sb[:, 4 * c + k:4 * c + k + 1] for k in range(4)]
                        bias = cbx_sb[:, c:c + 1]; pr = 128
                        dst = xbf_sb[:, c * T1:(c + 1) * T1]
                        ekey = f"evac_cvx{c}"
                    elif c == 16:
                        buf = cvb_sb; taps = [cwb_sb[:, k:k + 1] for k in range(4)]
                        bias = cbb_sb; pr = 16; dst = bbf_sb; ekey = "evac_cvB"
                    else:
                        buf = cvc_sb; taps = [cwc_sb[:, k:k + 1] for k in range(4)]
                        bias = cbc_sb; pr = 16; dst = cbf_sb; ekey = "evac_cvC"
                    ap = c % 4
                    acc = cacc_sb[0:pr, ap * T1:(ap + 1) * T1]
                    info[c] = (buf, taps, bias, pr, dst, ekey, acc, ap)
                for c in grp:
                    buf, taps, bias, pr, dst, ekey, acc, ap = info[c]
                    b.use("vector", f"cacc{ap}")
                    b.use("vector", ekey)
                    b.op("vector", lambda e, buf=buf, acc=acc, t=taps[0], pr=pr: e.tensor_scalar_mul(
                        out=acc, in0=buf[0:pr, 0:T1], scalar1=t))
                for k in range(1, 4):
                    vf = b.mark("vector")
                    b.wait("vector", "DVE", vf)
                    for c in grp:
                        buf, taps, bias, pr, dst, ekey, acc, ap = info[c]
                        b.op("vector", lambda e, buf=buf, acc=acc, t=taps[k], k=k, pr=pr: e.scalar_tensor_tensor(
                            out=acc, in0=buf[0:pr, k:k + T1], scalar=t, in1=acc, op0=ALU.mult, op1=ALU.add))
                vcv = b.mark("vector")
                for c in grp:
                    b.rel(f"convdve{c}", "DVE", vcv)
                b.wait("scalar", "DVE", vcv)
                for c in grp:
                    buf, taps, bias, pr, dst, ekey, acc, ap = info[c]
                    b.use("scalar", f"xb{c}")
                    vsl = b.op("scalar", lambda e, acc=acc, dst=dst, bias=bias: e.activation(
                        out=dst, in_=acc, func=AF.Silu, bias=bias), inc="ACT")
                    b.rel(f"cacc{ap}", "ACT", vsl)
                    silu_done[c] = vsl

            for c in range(18):
                if c < 16:
                    buf = cvx_sb[:, c * (T1 + 3):(c + 1) * (T1 + 3)]; pr = 128; key = f"cvx{c}"
                elif c == 16:
                    buf = cvb_sb; pr = 16; key = "cvB"
                else:
                    buf = cvc_sb; pr = 16; key = "cvC"
                b.use("gpsimd", f"convdve{c}")
                vh = b.op("gpsimd", lambda e, buf=buf, pr=pr: e.tensor_copy(
                    out=buf[0:pr, 0:3], in_=buf[0:pr, T1:T1 + 3]), inc="POOL")
                b.rel(key, "POOL", vh)

            if p1_stage < 4:
                continue
            b.use("scalar", "dtk")
            b.wait("scalar", "PE", dt_vmm)
            vex0 = b.op("scalar", lambda e, p=dt_pdst: e.activation(out=dte_sb, in_=p, func=AF.Exp, bias=dtb_sb), inc="ACT")
            b.wait("scalar", "ACT", vex0)
            vdt = b.op("scalar", lambda e: e.activation(out=dt_sb, in_=dte_sb, func=AF.Ln, bias=1.0), inc="ACT")
            b.rel(("pA", "pB")[dt_rb], "ACT", vdt, 0, 512)
            b.wait("vector", "ACT", vdt)
            vddt = b.op("vector", lambda e: e.tensor_scalar_mul(out=ddt_sb, in0=dt_sb, scalar1=acol_sb), inc="DVE")

            for qq in range(T1 // Q):
                if p1_stage < 5:
                    break
                c0 = qq * Q
                cp = qq % 2
                b.use("vector", "s8")
                if qq == 0:
                    b.wait("vector", "DVE", vddt)
                vsc = b.op("vector", lambda e, c0=c0: e.tensor_tensor_scan(
                    out=s_sb, data0=ddt_sb[:, c0:c0 + Q], data1=zero8_sb, initial=0.0,
                    op0=ALU.add, op1=ALU.add), inc="DVE")
                b.wait("vector", "DVE", vsc)
                vs_ = b.op("vector", lambda e: e.tensor_scalar(
                    out=r_sb, in0=s_sb, scalar1=s_sb[:, Q - 1:Q], scalar2=-1.0,
                    op0=ALU.subtract, op1=ALU.mult), inc="DVE")
                b.wait("scalar", "DVE", vs_)
                b.use("scalar", "rows8")
                b.use("scalar", "ec8")
                b.op("scalar", lambda e: e.activation(out=exps_sb, in_=s_sb, func=AF.Exp))
                b.op("scalar", lambda e: e.activation(out=ens_sb, in_=s_sb, func=AF.Exp, scale=-1.0))
                b.op("scalar", lambda e: e.activation(out=er_sb, in_=r_sb, func=AF.Exp))
                vex = b.op("scalar", lambda e: e.activation(
                    out=ecol8_sb, in_=s_sb[:, Q - 1:Q], func=AF.Exp), inc="ACT")
                b.rel("s8", "ACT", vex)
                b.wait("vector", "ACT", vex)
                b.use("vector", "nq8")
                b.op("vector", lambda e, c0=c0: e.tensor_mul(out=nsdt_sb, in0=ens_sb, in1=dt_sb[:, c0:c0 + Q]))
                b.op("vector", lambda e, c0=c0: e.tensor_mul(out=qf_sb, in0=er_sb, in1=dt_sb[:, c0:c0 + Q]))
                b.use("vector", "diage")
                vrows = b.op("vector", lambda e: e.tensor_scalar_mul(
                    out=diage_sb, in0=cons_sb[0:NH, 0:NH], scalar1=ecol8_sb), inc="DVE")
                b.rel("ec8", "DVE", vrows)
                b.wait("tensor", "DVE", vrows)
                b.use("tensor", "pE", 0, 512)
                b.op("tensor", lambda e: e.transpose(pE[:, 128:136], nsdt_sb, IDE[0:NH, 0:NH]))
                b.op("tensor", lambda e: e.transpose(pE[:, 136:144], qf_sb, IDE[0:NH, 0:NH]))
                b.op("tensor", lambda e: e.transpose(pE[:, 144:152], exps_sb, IDE[0:NH, 0:NH]))
                b.op("tensor", lambda e: e.matmul(pE[0:16, 152:160], ONES8x16, diage_sb,
                                                  start=True, stop=True))
                b.wait("tensor", "ACT", silu_done[17])
                b.op("tensor", lambda e, c0=c0: e.matmul(
                    pE[:, 160:176], bbf_sb[:, c0:c0 + Q], IDE_BF[0:16, 0:16],
                    start=True, stop=True))
                vg = b.op("tensor", lambda e, c0=c0: e.matmul(
                    pE[:, 0:128], bbf_sb[:, c0:c0 + Q], cbf_sb[:, c0:c0 + Q],
                    start=True, stop=True), inc="PE")
                b.rel("rows8", "PE", vg)
                b.rel("nq8", "PE", vg)
                b.rel("diage", "PE", vg)
                b.wait("vector", "PE", vg)
                b.op("vector", lambda e: e.tensor_copy(out=tm_sb, in_=pE[:, 128:152]))
                b.op("vector", lambda e: e.tensor_copy(out=ecol16_sb, in_=pE[0:16, 152:160]))
                b.use("vector", f"btm{cp}")
                b.op("vector", lambda e, cp=cp: e.tensor_copy(
                    out=btm_sb[:, cp * 16:(cp + 1) * 16], in_=pE[:, 160:176]))
                b.use("vector", f"gm{cp}")
                vgm = b.op("vector", lambda e, cp=cp: e.tensor_mul(
                    out=gm_sb[:, cp * 128:(cp + 1) * 128], in0=pE[:, 0:128], in1=TRIU), inc="DVE")
                b.rel("pE", "DVE", vgm, 0, 176)

                xtmo = cp * 2048
                b.use("scalar", f"xtm{cp}")
                for cidx in range(16):
                    bank = pX1 if cidx < 8 else pX2
                    po = (cidx % 8) * 128
                    b.use("tensor", "pX1" if cidx < 8 else "pX2", po, po + 128)
                    b.wait("tensor", "ACT", silu_done[cidx])
                    b.op("tensor", lambda e, cidx=cidx, c0=c0, bank=bank, po=po: e.transpose(
                        bank[:, po:po + 128],
                        xbf_sb[:, cidx * T1 + c0:cidx * T1 + c0 + Q], IDE_BF))
                vtp = b.mark("tensor")
                b.wait("scalar", "PE", vtp)
                for pair in range(8):
                    bank = pX1 if pair < 4 else pX2
                    po = (pair % 4) * 256
                    vev = b.op("scalar", lambda e, pair=pair, bank=bank, po=po, xtmo=xtmo: e.copy(
                        out=xtm_sb[:, xtmo + pair * 256:xtmo + (pair + 1) * 256],
                        in_=bank[:, po:po + 256]), inc="ACT")
                vxtm = vev
                if debug:
                    b.wait("gpsimd", "ACT", vxtm)
                    b.dma("gpsimd", xtm_dbg[t0 + c0:t0 + c0 + Q, :], xtm_sb[:, xtmo:xtmo + 2048], "ST")
                b.rel("pX1", "ACT", vxtm, 0, 1024)
                b.rel("pX2", "ACT", vxtm, 0, 1024)
                b.wait("tensor", "ACT", vxtm)
                b.wait("vector", "ACT", vxtm)

                yslot = [(pY, 0), (pY, 256), (pA, 0), (pA, 256),
                         (pB, 0), (pB, 256), (pC, 0), (pC, 256)]
                vdl2 = {}
                for h in range(NH):
                    xtm_h = xtm_sb[:, xtmo + h * 256:xtmo + (h + 1) * 256]
                    wslot = wp_sb[:, (cp * 8 + h) * 128:(cp * 8 + h + 1) * 128]
                    b.use("vector", f"wp{cp}{h}")
                    if h == 0:
                        b.wait("vector", "DVE", vgm)
                    vwp = b.op("vector", lambda e, wslot=wslot, cp=cp, h=h: e.tensor_scalar_mul(
                        out=wslot, in0=gm_sb[:, cp * 128:(cp + 1) * 128],
                        scalar1=tm_sb[:, h:h + 1]), inc="DVE")
                    b.wait("tensor", "DVE", vwp)
                    ybank, yo = yslot[h]
                    ytag = {id(pY): "pY", id(pA): "pA", id(pB): "pB", id(pC): "pC"}[id(ybank)]
                    b.use("tensor", ytag, yo, yo + 256)
                    b.op("tensor", lambda e, ybank=ybank, yo=yo, wslot=wslot, xtm_h=xtm_h: e.matmul(
                        ybank[:, yo:yo + 256], wslot, xtm_h, start=True, stop=False))
                    vy = b.op("tensor", lambda e, ybank=ybank, yo=yo, h=h, c0=c0: e.matmul(
                        ybank[:, yo:yo + 256], cbf_sb[:, c0:c0 + Q],
                        hbf_sb[:, h * 256:(h + 1) * 256], start=False, stop=True), inc="PE")
                    b.rel(f"wp{cp}{h}", "PE", vy)
                    b.wait("vector", "PE", vy)
                    b.use("vector", f"y1{h % 2}")
                    vy1 = b.op("vector", lambda e, ybank=ybank, yo=yo, h=h: e.tensor_scalar_mul(
                        out=y1_sb[:, (h % 2) * 256:(h % 2 + 1) * 256], in0=ybank[:, yo:yo + 256],
                        scalar1=tm_sb[:, 16 + h:17 + h]), inc="DVE")
                    xqs = xq_sb[:, (h % 4) * 256:(h % 4 + 1) * 256]
                    b.use("vector", f"xq{(h // 2) % 2}") if h % 2 == 0 else None
                    b.op("vector", lambda e, xqs=xqs, h=h, xtm_h=xtm_h: e.tensor_scalar_mul(
                        out=xqs, in0=xtm_h, scalar1=tm_sb[:, 8 + h:9 + h]))
                    b.wait("vector", "DVE", vy1)
                    vyt = b.op("vector", lambda e, h=h, xtm_h=xtm_h: e.scalar_tensor_tensor(
                        out=ytm_sb[:, h * 256:(h + 1) * 256], in0=xtm_h, scalar=dtm_sb[:, h:h + 1],
                        in1=y1_sb[:, (h % 2) * 256:(h % 2 + 1) * 256],
                        op0=ALU.mult, op1=ALU.add), inc="DVE")
                    if debug and it == 0 and qq == 0 and h == 0:
                        vypc = b.op("vector", lambda e, ybank=ybank, yo=yo: e.tensor_copy(
                            out=cacc_sb[:, 0:256], in_=ybank[:, yo:yo + 256]), inc="DVE")
                        b.wait("gpsimd", "DVE", vypc)
                        b.dma("gpsimd", yp_dbg, cacc_sb[:, 0:256], "ST")
                        b.dma("gpsimd", y1_dbg, y1_sb[:, 0:256], "ST")
                    b.rel(ytag, "DVE", vyt, yo, yo + 256)
                    b.rel(f"y1{h % 2}", "DVE", vyt)
                    b.wait("tensor", "DVE", vyt)
                    for j in range(2):
                        cidx = 2 * h + j
                        bank = pX1 if cidx < 8 else pX2
                        po = (cidx % 8) * 128
                        b.use("tensor", "pX1" if cidx < 8 else "pX2", po, po + 128)
                        vbtp = b.op("tensor", lambda e, h=h, j=j, bank=bank, po=po: e.transpose(
                            bank[:, po:po + 128],
                            ytm_sb[:, h * 256 + j * 128:h * 256 + j * 128 + 128],
                            IDE_BF), inc="PE")
                        b.wait("scalar", "PE", vbtp)
                        b.use("scalar", "yfmk")
                        vyf = b.op("scalar", lambda e, cidx=cidx, c0=c0, bank=bank, po=po: e.copy(
                            out=yfm_sb[:, cidx * T1 + c0:cidx * T1 + c0 + Q],
                            in_=bank[:, po:po + 128]), inc="ACT")
                        if debug and it == 0 and qq == 0 and h == 0:
                            b.wait("gpsimd", "ACT", vyf)
                            b.dma("gpsimd", px_dbg[:, j * 128:(j + 1) * 128],
                                  yfm_sb[:, cidx * T1 + c0:cidx * T1 + c0 + Q], "ST")
                        b.rel("pX1" if cidx < 8 else "pX2", "ACT", vyf, po, po + 128)
                    if h % 2 == 1:
                        vxq = b.mark("vector")
                        b.wait("tensor", "DVE", vxq)
                        dslot = [(pE, 0), (pS, 0), (pE, 0), (pS, 0)][(h // 2)]
                        dbank, do = dslot
                        dtag = "pE" if dbank is pE else "pS"
                        b.use("tensor", dtag, do, do + 512)
                        vdl = b.op("tensor", lambda e, dbank=dbank, do=do, cp=cp, h=h: e.matmul(
                            dbank[0:16, do:do + 512], btm_sb[:, cp * 16:(cp + 1) * 16],
                            xq_sb[:, ((h - 1) % 4) * 256:((h - 1) % 4 + 2) * 256],
                            start=True, stop=True), inc="PE")
                        b.wait("vector", "PE", vdl)
                        for hh in (h - 1, h):
                            vhn = b.op("vector", lambda e, hh=hh, dbank=dbank, do=do, h=h: e.scalar_tensor_tensor(
                                out=h_sb[:, hh * 256:(hh + 1) * 256],
                                in0=h_sb[:, hh * 256:(hh + 1) * 256],
                                scalar=ecol16_sb[:, hh:hh + 1],
                                in1=dbank[0:16, do + (hh - (h - 1)) * 256:do + (hh - (h - 1)) * 256 + 256],
                                op0=ALU.mult, op1=ALU.add), inc="DVE")
                        b.rel(dtag, "DVE", vhn, do, do + 512)
                        b.rel(f"xq{(h // 2) % 2}", "PE", vdl)
                        b.wait("scalar", "DVE", vhn)
                        for hh in (h - 1, h):
                            vhb = b.op("scalar", lambda e, hh=hh: e.copy(
                                out=hbf_sb[:, hh * 256:(hh + 1) * 256],
                                in_=h_sb[:, hh * 256:(hh + 1) * 256]), inc="ACT")
                b.wait("tensor", "ACT", vhb)
                if debug:
                    vdd = b.mark("vector")
                    b.wait("gpsimd", "DVE", vdd)
                    b.dma("gpsimd", ytm_dbg[t0 + c0:t0 + c0 + Q, :], ytm_sb[:, 0:2048], "ST")
                    if it == 0:
                        b.wait("gpsimd", "ACT", b.cnt["ACT"])
                        b.dma("gpsimd", (yf1_dbg if qq == 0 else yf2_dbg)[:, 0:128],
                              yfm_sb[:, c0:c0 + 128], "ST")
                b.rel(f"xtm{cp}", "PE", b.cnt["PE"])
                b.rel(f"xtm{cp}", "DVE", b.cnt["DVE"])
                b.rel(f"btm{cp}", "PE", b.cnt["PE"])
                b.rel(f"gm{cp}", "DVE", b.cnt["DVE"])
            b.rel("dtk", "DVE", b.cnt["DVE"])
            for c in range(18):
                b.rel(f"xb{c}", "PE", b.cnt["PE"])

            if p1_stage < 6:
                continue
            import os as _os2
            vyfm = b.cnt["ACT"]
            if debug:
                b.wait("gpsimd", "ACT", vyfm)
                if it == 0:
                    b.dma("gpsimd", yf3_dbg, yfm_sb[:, 0:256], "ST")
                for g in range(16):
                    b.dma("gpsimd", y_dbg[g * 128:(g + 1) * 128, t0:t0 + T1],
                          yfm_sb[:, g * T1:(g + 1) * T1], "ST")
                    b.dma("gpsimd", xbf_dbg[g * 128:(g + 1) * 128, t0:t0 + T1],
                          xbf_sb[:, g * T1:(g + 1) * T1], "ST")
            b.use("tensor", "pS", 0, 2 * T1)
            b.use("tensor", "pY", 0, T1)
            b.wait("tensor", "ACT", vyfm)
            b.wait("scalar", "ACT", vyfm)
            b.use("scalar", "yfmk")
            pend_y = None
            ng_y = 16 if not _os2.environ.get("NOSTATS") else 0
            for g in range(ng_y):
                ych = yfm_sb[:, g * T1:(g + 1) * T1]
                b.op("tensor", lambda e, ych=ych, g=g: e.matmul(
                    pS[0:1, 0:T1], ONES_COL_BF, ych, start=(g == 0), stop=(g == 15)))
                sq = g % 2
                b.use("scalar", f"ysq{sq}")
                vsq = b.op("scalar", lambda e, ych=ych, sq=sq: e.activation(
                    out=ysq_sb[:, sq * T1:(sq + 1) * T1], in_=ych, func=AF.Square), inc="ACT")
                for pg, pvsq in ([pend_y] if pend_y else []) + ([(g, vsq)] if g == ng_y - 1 else []):
                    psq = pg % 2
                    b.wait("tensor", "ACT", pvsq)
                    vqm = b.op("tensor", lambda e, psq=psq, pg=pg: e.matmul(
                        pY[0:1, 0:T1], ONES_COL_BF, ysq_sb[:, psq * T1:(psq + 1) * T1],
                        start=(pg == 0), stop=(pg == 15)), inc="PE")
                    b.rel(f"ysq{psq}", "PE", vqm)
                pend_y = (g, vsq)
            vyst = b.mark("tensor")
            if debug and it == 0:
                b.wait("gpsimd", "PE", vyst)
                b.dma("gpsimd", yf4_dbg, yfm_sb[:, 0:256], "ST")
            b.wait("vector", "PE", vyst)
            b.op("vector", lambda e: e.tensor_scalar_mul(out=row_a, in0=pS[0:1, 0:T1], scalar1=1.0 / DIN))
            vrb2 = b.op("vector", lambda e: e.tensor_scalar_mul(out=row_b, in0=pY[0:1, 0:T1], scalar1=1.0 / DIN), inc="DVE")
            b.wait("vector", "DVE", vrb2)
            vrc2 = b.op("vector", lambda e: e.tensor_mul(out=row_c, in0=row_a, in1=row_a), inc="DVE")
            b.wait("vector", "DVE", vrc2)
            vv2 = b.op("vector", lambda e: e.tensor_sub(out=row_b, in0=row_b, in1=row_c), inc="DVE")
            b.rel("pS", "DVE", vv2, 0, 2 * T1)
            b.rel("pY", "DVE", vv2, 0, T1)
            b.wait("scalar", "DVE", vv2)
            vln2 = b.op("scalar", lambda e: e.activation(out=row_b, in_=row_b, func=AF.Ln, bias=eps_sb[0:1, 0:1]), inc="ACT")
            b.wait("scalar", "ACT", vln2)
            vrsy = b.op("scalar", lambda e: e.activation(out=row_b, in_=row_b, func=AF.Exp, scale=-0.5), inc="ACT")
            b.wait("vector", "ACT", vrsy)
            vmr = b.op("vector", lambda e: e.scalar_tensor_tensor(
                out=row_c, in0=row_a, scalar=-1.0, in1=row_b, op0=ALU.mult, op1=ALU.mult), inc="DVE")
            b.wait("tensor", "DVE", vmr)
            b.use("tensor", "pC", 0, 2 * T1)
            b.op("tensor", lambda e: e.matmul(pC[:, 0:T1], ONES_ROW1, row_b, start=True, stop=True))
            vbc2 = b.op("tensor", lambda e: e.matmul(pC[:, T1:2 * T1], ONES_ROW1, row_c,
                                                     start=True, stop=True), inc="PE")
            b.wait("vector", "PE", vbc2)
            b.op("vector", lambda e: e.tensor_copy(out=ryb_sb, in_=pC[:, 0:T1]))
            vmrb = b.op("vector", lambda e: e.tensor_copy(out=mrb_sb, in_=pC[:, T1:2 * T1]), inc="DVE")
            b.rel("pC", "DVE", vmrb, 0, 2 * T1)
            b.wait("vector", "DVE", vmrb)
            for gg in range(0, 16 if not _os2.environ.get("NOYLN") else 0, 4):
                for g in range(gg, gg + 4):
                    lp = g % 4
                    ych = yfm_sb[:, g * T1:(g + 1) * T1]
                    dst = yln_sb[:, lp * T1:(lp + 1) * T1]
                    b.use("vector", f"yln{lp}")
                    b.op("vector", lambda e, ych=ych, dst=dst: e.tensor_mul(out=dst, in0=ych, in1=ryb_sb))
                vf1 = b.mark("vector")
                b.wait("vector", "DVE", vf1)
                for g in range(gg, gg + 4):
                    dst = yln_sb[:, (g % 4) * T1:(g % 4 + 1) * T1]
                    b.op("vector", lambda e, dst=dst: e.tensor_add(out=dst, in0=dst, in1=mrb_sb))
                vf2 = b.mark("vector")
                b.wait("vector", "DVE", vf2)
                for g in range(gg, gg + 4):
                    dst = yln_sb[:, (g % 4) * T1:(g % 4 + 1) * T1]
                    vy3 = b.op("vector", lambda e, dst=dst, g=g: e.tensor_scalar(
                        out=dst, in0=dst, scalar1=lng_sb[:, g:g + 1], scalar2=lnbt_sb[:, g:g + 1],
                        op0=ALU.mult, op1=ALU.add), inc="DVE")
                    b.wait("gpsimd", "DVE", vy3)
                    vsp = b.dma("gpsimd", yln_sp[g * 128:(g + 1) * 128, t0:t0 + T1], dst, "ST")
                    b.rel(f"yln{g % 4}", "ST", vsp)
            if debug and it == 0:
                b.wait("gpsimd", "DVE", b.cnt["DVE"])
                b.dma("gpsimd", yf5_dbg, yfm_sb[:, 0:256], "ST")
            b.rel("yfmk", "DVE", b.cnt["DVE"])
            b.rel("yfmk", "PE", vyst)

        b.barrier()
        b.flush()

    pws.close()
    if "a" not in phases:
        b.wait("sync", "ST", b.cnt["ST"])
        b.flush()
        ctx.close()
        return nc
    with ExitStack() as p2s:
        A = lambda n, shp, dt=F32: p2s.enter_context(nc.sbuf_tensor(n, list(shp), dt)).ap()
        wg_sb = A("wg_sb", [128, 16 * DIN], BF16)
        yl_sb = A("yl_sb", [128, 2 * 16 * T2], BF16)
        gs_sb = A("gs_sb", [128, 2 * T2], BF16)
        yg_sb = A("yg_sb", [128, 2 * T2], BF16)

        vwg = None
        for k in range(16):
            vwg = b.dma("sync", wg_sb[:, k * DIN:(k + 1) * DIN], wg_bf[k * 128:(k + 1) * 128, :], "LDP")

        def issue_yl(it2):
            pp = it2 % 2
            b.use("sync", f"ylt{pp}")
            v = None
            for g in range(16):
                v = b.dma("sync", yl_sb[:, (pp * 16 + g) * T2:(pp * 16 + g + 1) * T2],
                          yln_sp[g * 128:(g + 1) * 128, it2 * T2:(it2 + 1) * T2], f"LT{pp}")
            return v

        vyl_t = {0: issue_yl(0)}
        for it in range(nt2):
            t0 = it * T2
            pp = it % 2
            if it + 1 < nt2:
                vyl_t[it + 1] = issue_yl(it + 1)
            ylo = pp * 16 * T2
            b.wait("tensor", "LDP", vwg)
            b.wait("tensor", f"LT{pp}", vyl_t[it])
            b.wait("vector", f"LT{pp}", vyl_t[it])
            for m in range(16):
                rb = m % 2
                pdst = [pA, pB][rb]
                b.use("tensor", f"mm{rb}")
                for k in range(16):
                    b.op("tensor", lambda e, pdst=pdst, k=k, m=m, ylo=ylo: e.matmul(
                        pdst[:, 0:T2], wg_sb[:, k * DIN + m * 128:k * DIN + (m + 1) * 128],
                        yl_sb[:, ylo + k * T2:ylo + (k + 1) * T2], start=(k == 0), stop=(k == 15)))
                vmm = b.mark("tensor")
                b.wait("scalar", "PE", vmm)
                b.use("scalar", f"gs{rb}")
                vsg = b.op("scalar", lambda e, pdst=pdst, rb=rb, m=m: e.activation(
                    out=gs_sb[:, rb * T2:(rb + 1) * T2], in_=pdst[:, 0:T2], func=AF.Sigmoid,
                    bias=bgat_sb[:, m:m + 1]), inc="ACT")
                b.rel(f"mm{rb}", "ACT", vsg)
                b.wait("vector", "ACT", vsg)
                b.use("vector", f"yg{rb}")
                vyg = b.op("vector", lambda e, rb=rb, m=m, ylo=ylo: e.tensor_mul(
                    out=yg_sb[:, rb * T2:(rb + 1) * T2], in0=gs_sb[:, rb * T2:(rb + 1) * T2],
                    in1=yl_sb[:, ylo + m * T2:ylo + (m + 1) * T2]), inc="DVE")
                b.rel(f"gs{rb}", "DVE", vyg)
                b.wait("gpsimd", "DVE", vyg)
                vst = b.dma("gpsimd", ygat_sp[m * 128:(m + 1) * 128, t0:t0 + T2],
                            yg_sb[:, rb * T2:(rb + 1) * T2], "ST")
                b.rel(f"yg{rb}", "ST", vst)
            b.rel(f"ylt{pp}", "DVE", b.cnt["DVE"])
        b.barrier()
        b.flush()

    if "b" not in phases:
        b.wait("sync", "ST", b.cnt["ST"])
        b.flush()
        ctx.close()
        return nc
    with ExitStack() as p2bs:
        A = lambda n, shp, dt=F32: p2bs.enter_context(nc.sbuf_tensor(n, list(shp), dt)).ap()
        wo_sb = A("wo_sb", [128, 32 * D], BF16)
        ml_sb = A("ml_sb", [128, 2 * 16 * T2], BF16)
        yg2_sb = A("yg2_sb", [128, 2 * 16 * T2], BF16)
        u3_sb = A("u3_sb", [128, 2 * 8 * T2])
        x1_sb = A("x1_sb", [128, 2 * T2])
        x1q_sb = A("x1q_sb", [128, 2 * T2], BF16)
        srow_sb = A("srow_sb", [1, 2 * T2])

        vwo = None
        for k in range(32):
            vwo = b.dma("sync", wo_sb[:, k * D:(k + 1) * D], wo_bf[k * 128:(k + 1) * 128, :], "LDP")

        def issue_p2b(it2):
            pp = it2 % 2
            b.use("sync", f"p2bt{pp}")
            tt = it2 * T2
            v = None
            for g in range(16):
                v = b.dma("sync", ml_sb[:, (pp * 16 + g) * T2:(pp * 16 + g + 1) * T2],
                          mlp_sp[g * 128:(g + 1) * 128, tt:tt + T2], f"LT{pp}")
                v = b.dma("sync", yg2_sb[:, (pp * 16 + g) * T2:(pp * 16 + g + 1) * T2],
                          ygat_sp[g * 128:(g + 1) * 128, tt:tt + T2], f"LT{pp}")
            for g in range(8):
                v = b.dma("sync", u3_sb[:, (pp * 8 + g) * T2:(pp * 8 + g + 1) * T2],
                          uT[g * 128:(g + 1) * 128, tt:tt + T2], f"LT{pp}")
            return v

        vld_t = {0: issue_p2b(0)}
        for it in range(nt2):
            t0 = it * T2
            pp = it % 2
            if it + 1 < nt2:
                vld_t[it + 1] = issue_p2b(it + 1)
            mlo = pp * 16 * T2
            uo = pp * 8 * T2
            b.wait("tensor", "LDP", vwo)
            b.wait("tensor", f"LT{pp}", vld_t[it])
            b.wait("vector", f"LT{pp}", vld_t[it])
            b.use("tensor", "pS", 0, T2)
            b.use("tensor", "pY", 0, T2)
            pend_b = None
            for m in range(8):
                rb = m % 2
                pdst = [pA, pB][rb]
                b.use("tensor", f"mm{rb}")
                for k in range(32):
                    rhs = (ml_sb[:, mlo + k * T2:mlo + (k + 1) * T2] if k < 16
                           else yg2_sb[:, mlo + (k - 16) * T2:mlo + (k - 15) * T2])
                    b.op("tensor", lambda e, pdst=pdst, k=k, m=m, rhs=rhs: e.matmul(
                        pdst[:, 0:T2], wo_sb[:, k * D + m * 128:k * D + (m + 1) * 128], rhs,
                        start=(k == 0), stop=(k == 31)))
                vmm = b.mark("tensor")
                b.wait("vector", "PE", vmm)
                b.use("vector", f"x1{rb}")
                vx1 = b.op("vector", lambda e, pdst=pdst, rb=rb, m=m, uo=uo: e.tensor_add(
                    out=x1_sb[:, rb * T2:(rb + 1) * T2], in0=pdst[:, 0:T2],
                    in1=u3_sb[:, uo + m * T2:uo + (m + 1) * T2]), inc="DVE")
                b.rel(f"mm{rb}", "DVE", vx1)
                b.wait("gpsimd", "DVE", vx1)
                vst = b.dma("gpsimd", x1_sp[m * 128:(m + 1) * 128, t0:t0 + T2],
                            x1_sb[:, rb * T2:(rb + 1) * T2], "ST")
                b.wait("scalar", "DVE", vx1)
                b.use("scalar", f"x1q{rb}")
                vsq = b.op("scalar", lambda e, rb=rb: e.activation(
                    out=x1q_sb[:, rb * T2:(rb + 1) * T2], in_=x1_sb[:, rb * T2:(rb + 1) * T2],
                    func=AF.Square), inc="ACT")
                b.rel(f"x1{rb}", "ST", vst)
                for pm, pvx1, pvsq in ([pend_b] if pend_b else []) + ([(m, vx1, vsq)] if m == 7 else []):
                    prb = pm % 2
                    b.wait("tensor", "DVE", pvx1)
                    b.op("tensor", lambda e, prb=prb, pm=pm: e.matmul(
                        pS[0:1, 0:T2], ONES_COL, x1_sb[:, prb * T2:(prb + 1) * T2],
                        start=(pm == 0), stop=(pm == 7)))
                    b.wait("tensor", "ACT", pvsq)
                    vqm = b.op("tensor", lambda e, prb=prb, pm=pm: e.matmul(
                        pY[0:1, 0:T2], ONES_COL_BF, x1q_sb[:, prb * T2:(prb + 1) * T2],
                        start=(pm == 0), stop=(pm == 7)), inc="PE")
                    b.rel(f"x1q{prb}", "PE", vqm)
                pend_b = (m, vx1, vsq)
            vst8 = b.mark("tensor")
            b.wait("vector", "PE", vst8)
            b.use("vector", "srow")
            b.op("vector", lambda e: e.tensor_copy(out=srow_sb[0:1, 0:T2], in_=pS[0:1, 0:T2]))
            vsr = b.op("vector", lambda e: e.tensor_copy(out=srow_sb[0:1, T2:2 * T2], in_=pY[0:1, 0:T2]), inc="DVE")
            b.rel("pS", "DVE", vsr, 0, T2)
            b.rel("pY", "DVE", vsr, 0, T2)
            b.wait("gpsimd", "DVE", vsr)
            b.dma("gpsimd", st_sp[0:1, t0:t0 + T2], srow_sb[0:1, 0:T2], "ST")
            vst2 = b.dma("gpsimd", st_sp[1:2, t0:t0 + T2], srow_sb[0:1, T2:2 * T2], "ST")
            b.rel("srow", "ST", vst2)
            b.rel(f"p2bt{pp}", "PE", vst8)
            b.rel(f"p2bt{pp}", "DVE", vsr)
        b.barrier()
        b.flush()

    if "3" not in phases:
        b.wait("sync", "ST", b.cnt["ST"])
        b.flush()
        ctx.close()
        return nc
    with ExitStack() as p3s:
        A = lambda n, shp, dt=F32: p3s.enter_context(nc.sbuf_tensor(n, list(shp), dt)).ap()
        wfc_sb = A("wfc_sb", [128, 8 * H], BF16)
        wpr_sb = A("wpr_sb", [128, 32 * D], BF16)
        x14_sb = A("x14_sb", [128, 8 * T2])
        x1p_sb = A("x1p_sb", [128, 8 * T2], BF16)
        hr_sb = A("hr_sb", [128, 32 * T2], BF16)
        hsq_sb = A("hsq_sb", [128, 2 * T2], BF16)
        ot_sb = A("ot_sb", [128, 2 * T2])
        bcb_sb = A("bcb_sb", [128, 2 * T2], BF16)
        rw1_sb = A("rw1_sb", [1, T2])
        rw2_sb = A("rw2_sb", [1, T2])
        rw3_sb = A("rw3_sb", [1, T2])

        vw = None
        for k in range(8):
            vw = b.dma("sync", wfc_sb[:, k * H:(k + 1) * H], wfcq[k * 128:(k + 1) * 128, :], "LDP")
        for k in range(32):
            vw = b.dma("sync", wpr_sb[:, k * D:(k + 1) * D], wprq[k * 128:(k + 1) * 128, :], "LDP")
        for it in range(nt2):
            t0 = it * T2
            b.use("sync", "p3t")
            vld = None
            for g in range(8):
                vld = b.dma("sync", x14_sb[:, g * T2:(g + 1) * T2],
                            x1_sp[g * 128:(g + 1) * 128, t0:t0 + T2], "LDT")
            vr1 = b.dma("sync", rw1_sb, st_sp[0:1, t0:t0 + T2], "LDT")
            vr2 = b.dma("sync", rw2_sb, st_sp[1:2, t0:t0 + T2], "LDT")
            b.wait("vector", "LDT", vr2)
            b.wait("tensor", "LDP", vw)
            b.op("vector", lambda e: e.tensor_scalar_mul(out=rw1_sb, in0=rw1_sb, scalar1=1.0 / D))
            vwa = b.op("vector", lambda e: e.tensor_scalar_mul(out=rw2_sb, in0=rw2_sb, scalar1=1.0 / D), inc="DVE")
            b.wait("vector", "DVE", vwa)
            vwb = b.op("vector", lambda e: e.tensor_mul(out=rw3_sb, in0=rw1_sb, in1=rw1_sb), inc="DVE")
            b.wait("vector", "DVE", vwb)
            vv = b.op("vector", lambda e: e.tensor_sub(out=rw2_sb, in0=rw2_sb, in1=rw3_sb), inc="DVE")
            b.wait("scalar", "DVE", vv)
            vwl = b.op("scalar", lambda e: e.activation(out=rw2_sb, in_=rw2_sb, func=AF.Ln, bias=eps_sb[0:1, 0:1]), inc="ACT")
            b.wait("scalar", "ACT", vwl)
            vrs = b.op("scalar", lambda e: e.activation(out=rw2_sb, in_=rw2_sb, func=AF.Exp, scale=-0.5), inc="ACT")
            b.wait("vector", "ACT", vrs)
            vrh = b.op("vector", lambda e: e.tensor_scalar_mul(out=rw2_sb, in0=rw2_sb, scalar1=0.5), inc="DVE")
            b.wait("tensor", "DVE", vrh)
            b.use("tensor", "pC", 0, T2)
            b.use("tensor", "pE", 0, T2)
            b.op("tensor", lambda e: e.matmul(pC[:, 0:T2], ONES_ROW1, rw1_sb, start=True, stop=True))
            vbc = b.op("tensor", lambda e: e.matmul(pE[:, 0:T2], ONES_ROW1, rw2_sb,
                                                    start=True, stop=True), inc="PE")
            b.wait("vector", "PE", vbc)
            b.use("vector", "bcb")
            vrb = b.op("vector", lambda e: e.tensor_copy(out=bcb_sb[:, 0:T2], in_=pE[:, 0:T2]), inc="DVE")
            b.rel("pE", "DVE", vrb, 0, T2)
            b.wait("vector", "DVE", vrb)
            b.use("vector", "x1pall")
            for g in range(8):
                dst = x1p_sb[:, g * T2:(g + 1) * T2]
                b.op("vector", lambda e, g=g, dst=dst: e.tensor_sub(
                    out=dst, in0=x14_sb[:, g * T2:(g + 1) * T2], in1=pC[:, 0:T2]))
            vfx = b.mark("vector")
            b.wait("vector", "DVE", vfx)
            for g in range(8):
                dst = x1p_sb[:, g * T2:(g + 1) * T2]
                b.op("vector", lambda e, dst=dst: e.tensor_mul(out=dst, in0=dst, in1=bcb_sb[:, 0:T2]))
            vx1p = b.mark("vector")
            b.rel("pC", "DVE", vx1p, 0, T2)
            b.rel("bcb", "DVE", vx1p)
            b.wait("tensor", "DVE", vx1p)
            b.use("tensor", "pS", 0, T2)
            b.use("tensor", "pY", 0, T2)
            b.use("scalar", "hrall")
            pend_h = None
            for m in range(32):
                rb = m % 2
                pdst = [pA, pB][rb]
                b.use("tensor", f"mm{rb}")
                for k in range(8):
                    b.op("tensor", lambda e, pdst=pdst, k=k, m=m: e.matmul(
                        pdst[:, 0:T2], wfc_sb[:, k * H:(k + 1) * H][:, m * 128:(m + 1) * 128],
                        x1p_sb[:, k * T2:(k + 1) * T2], start=(k == 0), stop=(k == 7)))
                vmm = b.mark("tensor")
                b.wait("scalar", "PE", vmm)
                vgl = b.op("scalar", lambda e, pdst=pdst, m=m: e.activation(
                    out=hr_sb[:, m * T2:(m + 1) * T2], in_=pdst[:, 0:T2], func=AF.Gelu), inc="ACT")
                b.rel(f"mm{rb}", "ACT", vgl)
                sq = m % 2
                b.use("scalar", f"hsq{sq}")
                b.wait("scalar", "ACT", vgl)
                vsq = b.op("scalar", lambda e, m=m, sq=sq: e.activation(
                    out=hsq_sb[:, sq * T2:(sq + 1) * T2], in_=hr_sb[:, m * T2:(m + 1) * T2],
                    func=AF.Square), inc="ACT")
                for pm, pvgl, pvsq in ([pend_h] if pend_h else []) + ([(m, vgl, vsq)] if m == 31 else []):
                    psq = pm % 2
                    b.wait("tensor", "ACT", pvgl)
                    b.op("tensor", lambda e, pm=pm: e.matmul(
                        pS[0:1, 0:T2], ONES_COL_BF, hr_sb[:, pm * T2:(pm + 1) * T2],
                        start=(pm == 0), stop=(pm == 31)))
                    b.wait("tensor", "ACT", pvsq)
                    vqm = b.op("tensor", lambda e, psq=psq, pm=pm: e.matmul(
                        pY[0:1, 0:T2], ONES_COL_BF, hsq_sb[:, psq * T2:(psq + 1) * T2],
                        start=(pm == 0), stop=(pm == 31)), inc="PE")
                    b.rel(f"hsq{psq}", "PE", vqm)
                pend_h = (m, vgl, vsq)
            vst3 = b.mark("tensor")
            b.wait("vector", "PE", vst3)
            b.op("vector", lambda e: e.tensor_scalar_mul(out=rw1_sb, in0=pS[0:1, 0:T2], scalar1=1.0 / H))
            vwa3 = b.op("vector", lambda e: e.tensor_scalar_mul(out=rw2_sb, in0=pY[0:1, 0:T2], scalar1=1.0 / H), inc="DVE")
            b.wait("vector", "DVE", vwa3)
            vwb3 = b.op("vector", lambda e: e.tensor_mul(out=rw3_sb, in0=rw1_sb, in1=rw1_sb), inc="DVE")
            b.wait("vector", "DVE", vwb3)
            vv3 = b.op("vector", lambda e: e.tensor_sub(out=rw2_sb, in0=rw2_sb, in1=rw3_sb), inc="DVE")
            b.rel("pS", "DVE", vv3, 0, T2)
            b.rel("pY", "DVE", vv3, 0, T2)
            b.wait("scalar", "DVE", vv3)
            vwl3 = b.op("scalar", lambda e: e.activation(out=rw2_sb, in_=rw2_sb, func=AF.Ln, bias=eps_sb[0:1, 0:1]), inc="ACT")
            b.wait("scalar", "ACT", vwl3)
            vrs3 = b.op("scalar", lambda e: e.activation(out=rw2_sb, in_=rw2_sb, func=AF.Exp, scale=-0.5), inc="ACT")
            b.wait("vector", "ACT", vrs3)
            vrh3 = b.op("vector", lambda e: e.tensor_scalar_mul(out=rw2_sb, in0=rw2_sb, scalar1=0.5), inc="DVE")
            b.wait("tensor", "DVE", vrh3)
            b.use("tensor", "pC", 0, T2)
            b.use("tensor", "pE", 0, T2)
            b.op("tensor", lambda e: e.matmul(pC[:, 0:T2], ONES_ROW1, rw1_sb, start=True, stop=True))
            vbc3 = b.op("tensor", lambda e: e.matmul(pE[:, 0:T2], ONES_ROW1, rw2_sb,
                                                     start=True, stop=True), inc="PE")
            b.wait("vector", "PE", vbc3)
            b.use("vector", "bcb")
            b.op("vector", lambda e: e.tensor_copy(out=bcb_sb[:, 0:T2], in_=pC[:, 0:T2]))
            vmu3 = b.op("vector", lambda e: e.tensor_copy(out=bcb_sb[:, T2:2 * T2], in_=pE[:, 0:T2]), inc="DVE")
            b.rel("pC", "DVE", vmu3, 0, T2)
            b.rel("pE", "DVE", vmu3, 0, T2)
            b.wait("vector", "DVE", vmu3)
            for m in range(32):
                hch = hr_sb[:, m * T2:(m + 1) * T2]
                b.op("vector", lambda e, hch=hch: e.tensor_sub(out=hch, in0=hch, in1=bcb_sb[:, 0:T2]))
            vfh = b.mark("vector")
            b.wait("vector", "DVE", vfh)
            for m in range(32):
                hch = hr_sb[:, m * T2:(m + 1) * T2]
                b.op("vector", lambda e, hch=hch, m=m: e.scalar_tensor_tensor(
                    out=hch, in0=hch, scalar=flnw_sb[:, m:m + 1], in1=bcb_sb[:, T2:2 * T2],
                    op0=ALU.mult, op1=ALU.mult))
            vh2 = b.mark("vector")
            b.rel("bcb", "DVE", vh2)
            b.wait("tensor", "DVE", vh2)
            for m in range(8):
                rb = m % 2
                pdst = [pA, pB][rb]
                b.use("tensor", f"mm{rb}")
                for k in range(32):
                    b.op("tensor", lambda e, pdst=pdst, k=k, m=m: e.matmul(
                        pdst[:, 0:T2], wpr_sb[:, k * D:(k + 1) * D][:, m * 128:(m + 1) * 128],
                        hr_sb[:, k * T2:(k + 1) * T2], start=(k == 0), stop=False))
                b.op("tensor", lambda e, pdst=pdst, m=m: e.matmul(
                    pdst[:, 0:T2], cbrow_sb[0:1, m * 128:(m + 1) * 128], ONES_ROW_BF[0:1, 0:T2],
                    start=False, stop=True))
                vmm = b.mark("tensor")
                b.wait("vector", "PE", vmm)
                b.use("vector", f"ot{rb}")
                vo = b.op("vector", lambda e, pdst=pdst, rb=rb, m=m: e.tensor_add(
                    out=ot_sb[:, rb * T2:(rb + 1) * T2], in0=pdst[:, 0:T2],
                    in1=x14_sb[:, m * T2:(m + 1) * T2]), inc="DVE")
                b.rel(f"mm{rb}", "DVE", vo)
                b.wait("gpsimd", "DVE", vo)
                vst = b.dma("gpsimd", out_d[m * 128:(m + 1) * 128, t0:t0 + T2],
                            ot_sb[:, rb * T2:(rb + 1) * T2], "ST")
                b.rel(f"ot{rb}", "ST", vst)
            b.rel("hrall", "PE", b.cnt["PE"])
            b.rel("x1pall", "PE", b.cnt["PE"])
            b.rel("p3t", "PE", b.cnt["PE"])
            b.rel("p3t", "DVE", b.cnt["DVE"])

        b.wait("sync", "ST", b.cnt["ST"])
        b.flush()

    ctx.close()
    return nc


def host_inputs(inputs):
    f32 = np.float32
    WT = np.ascontiguousarray(np.asarray(inputs["L0_attn_W_in"], f32).T)
    cols = np.concatenate([np.arange(0, 4096), np.arange(6144, 8224), np.arange(8224, 8232)])
    winT = np.ascontiguousarray(WT[:, cols])
    cw = np.asarray(inputs["L0_attn_conv_w"], f32)[:, 0, :]
    cb = np.asarray(inputs["L0_attn_conv_b"], f32)
    shared = dict(
        winT=winT,
        wgT=np.ascontiguousarray(np.asarray(inputs["L0_attn_W_gate"], f32).T),
        woT=np.ascontiguousarray(np.asarray(inputs["L0_attn_W_out"], f32).T),
        wfcT=np.ascontiguousarray(np.asarray(inputs["L0_ffn_W_fc"], f32).T),
        wprT=np.ascontiguousarray(np.asarray(inputs["L0_ffn_W_proj"], f32).T),
        cwx=np.ascontiguousarray(cw[:2048].reshape(16, 128, 4).transpose(1, 0, 2).reshape(128, 64)),
        cwb=np.ascontiguousarray(cw[2048:2064]),
        cwc=np.ascontiguousarray(cw[2064:2080]),
        cbx=np.ascontiguousarray(cb[:2048].reshape(16, 128).T),
        cbb=np.ascontiguousarray(cb[2048:2064].reshape(16, 1)),
        cbc=np.ascontiguousarray(cb[2064:2080].reshape(16, 1)),
        dtb=np.ascontiguousarray(np.asarray(inputs["L0_attn_dt_bias"], f32).reshape(NH, 1)),
        alog=np.ascontiguousarray(np.asarray(inputs["L0_attn_A_log"], f32).reshape(NH, 1)),
        dpar=np.ascontiguousarray(np.asarray(inputs["L0_attn_D_param"], f32).reshape(NH, 1)),
        lng=np.ascontiguousarray(np.asarray(inputs["L0_attn_ln_gamma"], f32).reshape(16, 128).T),
        lnbt=np.ascontiguousarray(np.asarray(inputs["L0_attn_ln_beta"], f32).reshape(16, 128).T),
        bgat=np.ascontiguousarray(np.asarray(inputs["L0_attn_b_gate"], f32).reshape(16, 128).T),
        flnw=np.ascontiguousarray(np.asarray(inputs["L0_ffn_ln_w"], f32).reshape(32, 128).T),
        flnb=np.ascontiguousarray(np.asarray(inputs["L0_ffn_ln_b"], f32).reshape(32, 128).T),
    )
    consf = np.zeros((128, 768), f32)
    consf[:, 0:128] = np.eye(128, dtype=f32)
    consf[:, 128:656] = 1.0
    consbv = np.zeros((128, 768), np.float32)
    consbv[:, 0:128] = np.eye(128)
    consbv[:, 128:256] = np.triu(np.ones((128, 128)))
    consbv[:, 256:768] = 1.0
    shared["cons"] = consf
    shared["consb"] = consbv.astype(ml_dtypes.bfloat16)
    u = np.asarray(inputs["L0_attn_u"], f32)
    uTs = [np.ascontiguousarray(u[c].T) for c in range(8)]
    return shared, uTs


def kernel(**inputs):
    shared, uTs = host_inputs(inputs)
    nc = build_program(debug=False)
    in_maps = [dict(shared, uT=uTs[c]) for c in range(8)]
    res = run_bass_kernel_spmd(nc, in_maps, core_ids=list(range(8)))
    out = np.stack([np.asarray(r["out"]).T for r in res.results], axis=0)
    return out.astype(np.float32)


# version 08
# speedup vs baseline: 1.0232x; 1.0232x over previous
import numpy as np
import ml_dtypes
import concourse.bass as bass
import concourse.mybir as mybir
from concourse.bass_utils import run_bass_kernel_spmd
from contextlib import ExitStack

DT = mybir.dt
F32 = DT.float32
BF16 = DT.bfloat16
AF = mybir.ActivationFunctionType
ALU = mybir.AluOpType
AX = mybir.AxisListType

L, D, DIN, NH, HD, DS, Q, H = 4096, 1024, 2048, 8, 256, 16, 128, 4096
T1 = 256
NT1 = L // T1
T2 = 512
NT2 = L // T2
MUSE = 6184

ENGS = ("sync", "tensor", "vector", "scalar", "gpsimd")
E2S = {"tensor": "PE", "vector": "DVE", "scalar": "ACT", "gpsimd": "POOL"}


class Builder:
    def __init__(self, nc, ctx):
        self.nc = nc
        self.q = {e: [] for e in ENGS}
        self.sems = {}
        self.cnt = {}
        self.pending = {e: None for e in ENGS}
        for s in ("LDP", "LD0", "LD1", "LDG", "LDT", "LT0", "LT1", "ST", "ST2", "PE", "DVE", "ACT", "POOL"):
            self.sems[s] = ctx.enter_context(nc.semaphore(s))
            self.cnt[s] = 0
        self.regions = []
        self.block = ctx.enter_context(nc.Block())

    def op(self, eng, fn, inc=None):
        cell = [None]
        sems = self.sems

        def run(e, fn=fn, cell=cell):
            ins = fn(e)
            if cell[0] is not None:
                ins.then_inc(sems[cell[0][0]], cell[0][1])
        self.q[eng].append(run)
        if inc is not None:
            self.cnt[inc] += 1
            cell[0] = (inc, 1)
            self.pending[eng] = None
            return self.cnt[inc]
        self.pending[eng] = cell
        return None

    def mark(self, eng):
        sem = E2S[eng]
        cell = self.pending[eng]
        if cell is None:
            return self.cnt[sem]
        self.cnt[sem] += 1
        cell[0] = (sem, 1)
        self.pending[eng] = None
        return self.cnt[sem]

    def dma(self, eng, out, in_, sem):
        self.cnt[sem] += 16
        val = self.cnt[sem]
        s = self.sems[sem]

        def run(e, out=out, in_=in_, s=s):
            e.dma_start(out=out, in_=in_).then_inc(s, 16)
        self.q[eng].append(run)
        self.pending[eng] = None
        return val

    def wait(self, eng, sem, val):
        if not val:
            return
        s = self.sems[sem]

        def run(e, s=s, val=val):
            e.wait_ge(s, val)
        self.q[eng].append(run)

    def use(self, eng, tag, lo=0, hi=1 << 30):
        keep = []
        own = E2S.get(eng)
        for ent in self.regions:
            t, a, b_, sem, val = ent
            if t == tag and a < hi and lo < b_:
                if sem in ("ST", "ST2"):
                    self.wait(eng, sem, self.cnt[sem])
                else:
                    self.wait(eng, sem, val)
            else:
                keep.append(ent)
        self.regions = keep

    def rel(self, tag, sem, val, lo=0, hi=1 << 30):
        self.regions = [ent for ent in self.regions
                        if not (ent[0] == tag and ent[3] == sem
                                and lo <= ent[1] and ent[2] <= hi)]
        self.regions.append((tag, lo, hi, sem, val))

    def barrier(self):
        for e in E2S:
            self.mark(e)
        vals = {s: self.cnt[s] for s in ("PE", "DVE", "ACT", "POOL", "LDP", "LD0", "LD1", "LDG", "LDT", "LT0", "LT1", "ST", "ST2")}
        for e in ENGS:
            for s, v in vals.items():
                if v > 0:
                    self.wait(e, s, v)
        self.regions = []

    def flush(self):
        for eng in ENGS:
            ops = self.q[eng]
            self.q[eng] = []
            if not ops:
                continue
            getattr(self.block, eng)(lambda e, ops=ops: [f(e) for f in ops])


def build_program(debug=False, phases="012ab3", nt1=NT1, nt2=NT2, p1_stage=6, races=False):
    nc = bass.Bass("TRN2", target_bir_lowering=False, detect_race_conditions=races)
    ctx = ExitStack()

    def din(name, shape, dt=F32):
        return nc.dram_tensor(name, list(shape), dt, kind="ExternalInput").ap()

    uT = din("uT", [D, L])
    winT = din("winT", [D, MUSE])
    wgT = din("wgT", [DIN, DIN])
    woT = din("woT", [2 * DIN, D])
    wfcT = din("wfcT", [D, H])
    wprT = din("wprT", [H, D])
    cwx = din("cwx", [128, 64])
    cwb = din("cwb", [16, 4])
    cwc = din("cwc", [16, 4])
    cbx = din("cbx", [128, 16])
    cbb = din("cbb", [16, 1])
    cbc = din("cbc", [16, 1])
    dtb = din("dtb", [NH, 1])
    alog = din("alog", [NH, 1])
    dpar = din("dpar", [NH, 1])
    lng = din("lng", [128, 16])
    lnbt = din("lnbt", [128, 16])
    bgat = din("bgat", [128, 16])
    flnw = din("flnw", [128, 32])
    flnb = din("flnb", [128, 32])
    cons = din("cons", [128, 768])
    consb = din("consb", [128, 768], BF16)

    out_d = nc.dram_tensor("out", [D, L], F32, kind="ExternalOutput").ap()
    skind = "ExternalOutput" if debug else "Internal"
    wg_bf = nc.dram_tensor("wg_bf", [DIN, DIN], BF16, kind="Internal").ap()
    wo_bf = nc.dram_tensor("wo_bf", [2 * DIN, D], BF16, kind="Internal").ap()
    wfcq = nc.dram_tensor("wfcq", [D, H], BF16, kind=skind).ap()
    wprq = nc.dram_tensor("wprq", [H, D], BF16, kind="Internal").ap()
    yln_sp = nc.dram_tensor("yln_sp", [DIN, L], BF16, kind=skind).ap()
    mlp_sp = nc.dram_tensor("mlp_sp", [DIN, L], BF16, kind=skind).ap()
    ygat_sp = nc.dram_tensor("ygat_sp", [DIN, L], BF16, kind="Internal").ap()
    x1_sp = nc.dram_tensor("x1_sp", [D, L], F32, kind=skind).ap()
    st_sp = nc.dram_tensor("st_sp", [2, L], F32, kind=skind).ap()
    y_dbg = nc.dram_tensor("y_dbg", [DIN, L], BF16, kind=skind).ap() if debug else None
    xbf_dbg = nc.dram_tensor("xbf_dbg", [DIN, L], BF16, kind=skind).ap() if debug else None
    xtm_dbg = nc.dram_tensor("xtm_dbg", [L, DIN], BF16, kind=skind).ap() if debug else None
    ytm_dbg = nc.dram_tensor("ytm_dbg", [L, DIN], BF16, kind=skind).ap() if debug else None
    px_dbg = nc.dram_tensor("px_dbg", [128, 256], BF16, kind=skind).ap() if debug else None
    y1_dbg = nc.dram_tensor("y1_dbg", [128, 256], BF16, kind=skind).ap() if debug else None
    yp_dbg = nc.dram_tensor("yp_dbg", [128, 256], F32, kind=skind).ap() if debug else None
    yf1_dbg = nc.dram_tensor("yf1_dbg", [128, 256], BF16, kind=skind).ap() if debug else None
    yf2_dbg = nc.dram_tensor("yf2_dbg", [128, 256], BF16, kind=skind).ap() if debug else None
    yf3_dbg = nc.dram_tensor("yf3_dbg", [128, 256], BF16, kind=skind).ap() if debug else None
    yf4_dbg = nc.dram_tensor("yf4_dbg", [128, 256], BF16, kind=skind).ap() if debug else None
    yf5_dbg = nc.dram_tensor("yf5_dbg", [128, 256], BF16, kind=skind).ap() if debug else None

    b = Builder(nc, ctx)

    SB = lambda n, shp, dt=F32: nc.alloc_sbuf_tensor(n, list(shp), dt).ap()
    cons_sb = SB("cons_sb", [128, 768])
    consb_sb = SB("consb_sb", [128, 768], BF16)
    cbrow_sb = SB("cbrow_sb", [1, D], BF16)
    cwx_sb = SB("cwx_sb", [128, 64]); cwb_sb = SB("cwb_sb", [16, 4]); cwc_sb = SB("cwc_sb", [16, 4])
    cbx_sb = SB("cbx_sb", [128, 16]); cbb_sb = SB("cbb_sb", [16, 1]); cbc_sb = SB("cbc_sb", [16, 1])
    dtb_sb = SB("dtb_sb", [NH, 1]); alog_sb = SB("alog_sb", [NH, 1]); dpar_sb = SB("dpar_sb", [NH, 1])
    lng_sb = SB("lng_sb", [128, 16]); lnbt_sb = SB("lnbt_sb", [128, 16]); bgat_sb = SB("bgat_sb", [128, 16])
    flnw_sb = SB("flnw_sb", [128, 32]); flnb_sb = SB("flnb_sb", [128, 32])
    flnbh_sb = SB("flnbh_sb", [128, 32], BF16)
    acol_sb = SB("acol_sb", [NH, 1])
    dtm_sb = SB("dtm_sb", [128, NH])
    diag_sb = SB("diag_sb", [NH, NH])
    gcol_sb = SB("gcol_sb", [128, 4])
    grow_sb = SB("grow_sb", [1, 2])
    eps_sb = SB("eps_sb", [128, 1])

    IDE = cons_sb[:, 0:128]
    ONES_COL = cons_sb[:, 128:129]
    ONES_ROW1 = cons_sb[0:1, 128:256]
    ONES8x16 = cons_sb[0:NH, 128:144]
    ONES8x128 = cons_sb[0:NH, 128:256]
    IDE_BF = consb_sb[:, 0:128]
    TRIU = consb_sb[:, 128:256]
    ONES_COL_BF = consb_sb[:, 255:256]
    ONES_ROW_BF = consb_sb[0:1, 256:768]

    PS = lambda n, shp, dt=F32: nc.alloc_psum_tensor(n, list(shp), dt).ap()
    pA = PS("pA", [128, 512])
    pB = PS("pB", [128, 512])
    pC = PS("pC", [128, 512])
    pE = PS("pE", [128, 512])
    pS = PS("pS", [128, 512])
    pX1 = PS("pX1", [128, 1024], BF16)
    pX2 = PS("pX2", [128, 1024], BF16)
    pY = PS("pY", [128, 512])

    v = None
    for dst, src in [(cons_sb, cons), (consb_sb, consb), (cwx_sb, cwx), (cwb_sb, cwb),
                     (cwc_sb, cwc), (cbx_sb, cbx), (cbb_sb, cbb), (cbc_sb, cbc),
                     (dtb_sb, dtb), (alog_sb, alog), (dpar_sb, dpar), (lng_sb, lng),
                     (lnbt_sb, lnbt), (bgat_sb, bgat), (flnw_sb, flnw), (flnb_sb, flnb)]:
        v = b.dma("sync", dst, src, "LDP")
    b.wait("scalar", "LDP", v)
    b.op("scalar", lambda e: e.activation(out=acol_sb, in_=alog_sb, func=AF.Exp))
    va = b.op("scalar", lambda e: e.copy(out=flnbh_sb, in_=flnb_sb), inc="ACT")
    b.wait("vector", "ACT", va)
    b.op("vector", lambda e: e.tensor_scalar_mul(out=eps_sb, in0=cons_sb[:, 128:129], scalar1=1e-5))
    b.op("vector", lambda e: e.tensor_scalar_mul(out=acol_sb, in0=acol_sb, scalar1=-1.0))
    b.op("vector", lambda e: e.tensor_scalar_mul(out=flnbh_sb, in0=flnbh_sb, scalar1=0.5))
    vd = b.op("vector", lambda e: e.tensor_scalar_mul(
        out=diag_sb, in0=cons_sb[0:NH, 0:NH], scalar1=dpar_sb), inc="DVE")
    b.wait("tensor", "DVE", vd)
    vm = b.op("tensor", lambda e: e.matmul(pE[:, 0:NH], ONES8x128, diag_sb,
                                           start=True, stop=True), inc="PE")
    b.wait("vector", "PE", vm)
    vt = b.op("vector", lambda e: e.tensor_copy(out=dtm_sb, in_=pE[:, 0:NH]), inc="DVE")
    b.rel("pE", "DVE", vt, 0, 512)

    pws = ExitStack()
    win_sb = pws.enter_context(nc.sbuf_tensor("win_sb", [128, 8 * MUSE], BF16)).ap()
    with ExitStack() as p0s:
        A0 = lambda n, shp, dt=F32: p0s.enter_context(nc.sbuf_tensor(n, list(shp), dt)).ap()
        land_t = A0("p0_land", [128, 2 * 4096])
        bf_t = A0("p0_bf", [128, 2 * 4096], BF16)
        m_t = A0("p0_m", [128, 2 * 4096], BF16)
        g_t = A0("p0_g", [128, 48])

        LQ = ("sync", "gpsimd")
        LS = ("LD0", "LDG")

        def cast_weight(src, dst, rows, cols, tag):
            for k in range(rows // 128):
                par = k % 2
                land = land_t[:, par * 4096: par * 4096 + cols]
                bft = bf_t[:, par * 4096: par * 4096 + cols]
                b.use(LQ[par], f"p0L{par}")
                vl = b.dma(LQ[par], land, src[k * 128:(k + 1) * 128, :], LS[par])
                b.use("vector", f"p0B{par}")
                b.wait("vector", LS[par], vl)
                vc = b.op("vector", lambda e, land=land, bft=bft: e.tensor_copy(out=bft, in_=land), inc="DVE")
                b.rel(f"p0L{par}", "DVE", vc)
                b.wait("scalar", "DVE", vc)
                vs = b.dma("scalar", dst[k * 128:(k + 1) * 128, :], bft, "ST2")
                b.rel(f"p0B{par}", "ST2", vs)

        for k in range(8):
            for c0w, cw in ((0, 3092), (3092, MUSE - 3092)):
                par = (k * 2 + (0 if c0w == 0 else 1)) % 2
                land = land_t[:, par * 4096: par * 4096 + cw]
                b.use(LQ[par], f"p0L{par}")
                vl = b.dma(LQ[par], land, winT[k * 128:(k + 1) * 128, c0w:c0w + cw], LS[par])
                b.wait("vector", LS[par], vl)
                vc = b.op("vector", lambda e, land=land, k=k, c0w=c0w, cw=cw: e.tensor_copy(
                    out=win_sb[:, k * MUSE + c0w:k * MUSE + c0w + cw], in_=land), inc="DVE")
                b.rel(f"p0L{par}", "DVE", vc)
        cast_weight(wgT, wg_bf, DIN, DIN, "wg")
        cast_weight(woT, wo_bf, 2 * DIN, D, "wo")

        def quantize_weight(src, dst, rows, cols, gi, tag, with_cb=False):
            nk = rows // 128
            for k in range(nk):
                par = k % 2
                land = land_t[:, par * 4096: par * 4096 + cols]
                b.use(LQ[par], f"p0L{par}")
                vl = b.dma(LQ[par], land, src[k * 128:(k + 1) * 128, :], LS[par])
                b.wait("scalar", LS[par], vl)
                b.use("scalar", f"p0M{par}")
                mt = m_t[:, par * 4096: par * 4096 + cols]
                vr = b.op("scalar", lambda e, land=land, k=k, mt=mt: e.activation(
                    out=mt, in_=land, func=AF.Abs, accum_out=g_t[:, k:k + 1]), inc="ACT")
                b.rel(f"p0L{par}", "ACT", vr)
                b.rel(f"p0M{par}", "ACT", vr)
            b.wait("vector", "ACT", vr)
            vr2 = b.op("vector", lambda e, nk=nk: e.reduce_sum(
                out=g_t[:, 40:41], in_=g_t[:, 0:nk], axis=AX.X), inc="DVE")
            b.wait("tensor", "DVE", vr2)
            b.use("tensor", "pS", 0, 2)
            vm0 = b.op("tensor", lambda e: e.matmul(pS[0:1, 0:1], g_t[:, 40:41], ONES_COL,
                                                    start=True, stop=True), inc="PE")
            b.wait("vector", "PE", vm0)
            sc1 = 0.5 / (rows * cols)
            vc0 = b.op("vector", lambda e, sc1=sc1: e.tensor_scalar(
                out=grow_sb[0:1, 0:1], in0=pS[0:1, 0:1], scalar1=sc1, scalar2=0.5e-6,
                op0=ALU.mult, op1=ALU.add), inc="DVE")
            b.rel("pS", "DVE", vc0, 0, 2)
            b.wait("tensor", "DVE", vc0)
            b.use("tensor", "pS", 0, 2)
            vbc = b.op("tensor", lambda e: e.matmul(pS[:, 1:2], ONES_ROW1, grow_sb[0:1, 0:1],
                                                    start=True, stop=True), inc="PE")
            b.wait("vector", "PE", vbc)
            b.op("vector", lambda e, gi=gi: e.tensor_copy(out=gcol_sb[:, 2 * gi:2 * gi + 1], in_=pS[:, 1:2]))
            vng = b.op("vector", lambda e, gi=gi: e.tensor_scalar_mul(
                out=gcol_sb[:, 2 * gi + 1:2 * gi + 2], in0=pS[:, 1:2], scalar1=-1.0), inc="DVE")
            b.rel("pS", "DVE", vng, 0, 2)
            b.wait("scalar", "DVE", vng)
            vcb = None
            for k in range(nk):
                par = k % 2
                land = land_t[:, par * 4096: par * 4096 + cols]
                mt = m_t[:, par * 4096: par * 4096 + cols]
                bft = bf_t[:, par * 4096: par * 4096 + cols]
                b.use(LQ[par], f"p0L{par}")
                vl = b.dma(LQ[par], land, src[k * 128:(k + 1) * 128, :], LS[par])
                b.wait("scalar", LS[par], vl)
                b.use("scalar", f"p0M{par}")
                b.op("scalar", lambda e, land=land, mt=mt, gi=gi: e.activation(
                    out=mt, in_=land, func=AF.Sign, bias=gcol_sb[:, 2 * gi + 1:2 * gi + 2]))
                b.use("scalar", f"p0B{par}")
                vs2 = b.op("scalar", lambda e, land=land, bft=bft, gi=gi: e.activation(
                    out=bft, in_=land, func=AF.Sign, bias=gcol_sb[:, 2 * gi:2 * gi + 1]), inc="ACT")
                b.rel(f"p0L{par}", "ACT", vs2)
                b.wait("vector", "ACT", vs2)
                vq = b.op("vector", lambda e, mt=mt, bft=bft: e.tensor_add(out=bft, in0=bft, in1=mt), inc="DVE")
                b.rel(f"p0M{par}", "DVE", vq)
                if with_cb:
                    b.wait("tensor", "DVE", vq)
                    if k == 0:
                        b.use("tensor", "pY", 0, 512)
                        b.use("tensor", "pE", 0, 512)
                    for j2, bank in enumerate((pY, pE)):
                        b.op("tensor", lambda e, bft=bft, k=k, j2=j2, nk=nk, bank=bank: e.matmul(
                            bank[0:1, 0:512],
                            flnbh_sb[:, k:k + 1], bft[:, j2 * 512:(j2 + 1) * 512],
                            start=(k == 0), stop=(k == nk - 1)))
                    vcb = b.mark("tensor")
                    b.rel(f"p0B{par}", "PE", vcb)
                b.wait("gpsimd", "DVE", vq)
                vst = b.dma("gpsimd", dst[k * 128:(k + 1) * 128, :], bft, "ST")
                b.rel(f"p0B{par}", "ST", vst)
            if with_cb:
                b.wait("vector", "PE", vcb)
                b.op("vector", lambda e: e.tensor_copy(out=cbrow_sb[0:1, 0:512], in_=pY[0:1, 0:512]))
                vce = b.op("vector", lambda e: e.tensor_copy(out=cbrow_sb[0:1, 512:1024], in_=pE[0:1, 0:512]), inc="DVE")
                b.rel("pY", "DVE", vce, 0, 512)
                b.rel("pE", "DVE", vce, 0, 512)

        quantize_weight(wfcT, wfcq, D, H, 0, "fc")
        quantize_weight(wprT, wprq, H, D, 1, "pr", with_cb=True)
        b.barrier()
        b.flush()

    if "1" not in phases:
        b.wait("sync", "ST", b.cnt["ST"])
        b.flush()
        ctx.close()
        return nc
    with ExitStack() as p1s:
        A = lambda n, shp, dt=F32: p1s.enter_context(nc.sbuf_tensor(n, list(shp), dt)).ap()
        u_sb = A("u_sb", [128, 2 * 8 * T1])
        u2_sb = A("u2_sb", [128, 8 * T1], BF16)
        usq_sb = A("usq_sb", [128, 2 * T1], BF16)
        cvx_sb = A("cvx_sb", [128, 16 * (T1 + 3)], BF16)
        cvb_sb = A("cvb_sb", [16, T1 + 3], BF16)
        cvc_sb = A("cvc_sb", [16, T1 + 3], BF16)
        cacc_sb = A("cacc_sb", [128, 4 * T1])
        xbf_sb = A("xbf_sb", [128, 16 * T1], BF16)
        bbf_sb = A("bbf_sb", [16, T1], BF16)
        cbf_sb = A("cbf_sb", [16, T1], BF16)
        xtm_sb = A("xtm_sb", [128, 2 * 2048], BF16)
        btm_sb = A("btm_sb", [128, 2 * 16], BF16)
        gm_sb = A("gm_sb", [128, 2 * 128], BF16)
        wp_sb = A("wp_sb", [128, 2 * 8 * 128], BF16)
        xq_sb = A("xq_sb", [128, 4 * 256], BF16)
        y1_sb = A("y1_sb", [128, 2 * 256], BF16)
        ytm_sb = A("ytm_sb", [128, 8 * 256], BF16)
        yfm_sb = A("yfm_sb", [128, 16 * T1], BF16)
        ysq_sb = A("ysq_sb", [128, 2 * T1], BF16)
        mlp_sb = A("mlp_sb", [128, 4 * T1], BF16)
        z0s_sb = A("z0s_sb", [128, 2 * T1], BF16)
        yln_sb = A("yln_sb", [128, 4 * T1], BF16)
        dt_sb = A("dt_sb", [NH, T1])
        dte_sb = A("dte_sb", [NH, T1])
        ddt_sb = A("ddt_sb", [NH, T1])
        zero8_sb = A("zero8_sb", [NH, Q])
        s_sb = A("s_sb", [NH, Q])
        r_sb = A("r_sb", [NH, Q])
        exps_sb = A("exps_sb", [NH, Q])
        ens_sb = A("ens_sb", [NH, Q])
        er_sb = A("er_sb", [NH, Q])
        nsdt_sb = A("nsdt_sb", [NH, Q])
        qf_sb = A("qf_sb", [NH, Q])
        ecol8_sb = A("ecol8_sb", [NH, 1])
        diage_sb = A("diage_sb", [NH, NH])
        tm_sb = A("tm_sb", [128, 24])
        ecol16_sb = A("ecol16_sb", [16, NH])
        h_sb = A("h_sb", [16, 2048])
        hbf_sb = A("hbf_sb", [16, 2048], BF16)
        row_a = A("row_a", [1, T1])
        row_b = A("row_b", [1, T1])
        row_c = A("row_c", [1, T1])
        rs0b_sb = A("rs0b_sb", [128, T1], BF16)
        ryb_sb = A("ryb_sb", [128, T1], BF16)
        mrb_sb = A("mrb_sb", [128, T1], BF16)

        b.op("vector", lambda e: e.memset(zero8_sb, 0.0))
        b.op("vector", lambda e: e.memset(h_sb, 0.0))
        b.op("vector", lambda e: e.memset(hbf_sb, 0.0))
        b.op("vector", lambda e: e.memset(cvx_sb, 0.0))
        b.op("vector", lambda e: e.memset(cvb_sb, 0.0))
        vini = b.op("vector", lambda e: e.memset(cvc_sb, 0.0), inc="DVE")
        b.wait("scalar", "DVE", vini)
        b.wait("tensor", "DVE", vini)
        b.wait("gpsimd", "DVE", vini)

        mplan = []
        for g in range(16):
            mplan.append(("z0", g, g * 128, 128))
            mplan.append(("x0", g, 2048 + g * 128, 128))
        for g in range(16):
            mplan.append(("X", g, 4096 + g * 128, 128))
        mplan += [("B", 0, 6144, 16), ("C", 0, 6160, 16), ("dt", 0, 6176, 8)]

        uval = [None] * (NT1 + 2)
        for i in range(min(2, NT1)):
            par = i % 2
            for g in range(8):
                uval[i] = b.dma("sync", u_sb[:, (par * 8 + g) * T1:(par * 8 + g + 1) * T1],
                                uT[g * 128:(g + 1) * 128, i * T1:(i + 1) * T1], f"LD{par}")

        for it in range(nt1):
            par = it % 2
            uofs = par * 8 * T1
            t0 = it * T1

            b.wait("tensor", f"LD{par}", uval[it])
            b.wait("scalar", f"LD{par}", uval[it])
            b.use("tensor", "pS", 0, 2 * T1)
            b.use("tensor", "pY", 0, T1)
            pend_u = None
            for g in range(8):
                usrc = u_sb[:, uofs + g * T1: uofs + (g + 1) * T1]
                b.op("tensor", lambda e, usrc=usrc, g=g: e.matmul(
                    pS[0:1, 0:T1], ONES_COL, usrc, start=(g == 0), stop=(g == 7)))
                sq = g % 2
                b.use("scalar", f"usq{sq}")
                vsq = b.op("scalar", lambda e, usrc=usrc, sq=sq: e.activation(
                    out=usq_sb[:, sq * T1:(sq + 1) * T1], in_=usrc, func=AF.Square), inc="ACT")
                for pg, pvsq in ([pend_u] if pend_u else []) + ([(g, vsq)] if g == 7 else []):
                    psq = pg % 2
                    b.wait("tensor", "ACT", pvsq)
                    vqm = b.op("tensor", lambda e, psq=psq, pg=pg: e.matmul(
                        pY[0:1, 0:T1], ONES_COL_BF, usq_sb[:, psq * T1:(psq + 1) * T1],
                        start=(pg == 0), stop=(pg == 7)), inc="PE")
                    b.rel(f"usq{psq}", "PE", vqm)
                pend_u = (g, vsq)
            vstat = b.mark("tensor")
            b.wait("vector", "PE", vstat)
            b.op("vector", lambda e: e.tensor_scalar_mul(out=row_a, in0=pS[0:1, 0:T1], scalar1=1.0 / D))
            vrb = b.op("vector", lambda e: e.tensor_scalar_mul(out=row_b, in0=pY[0:1, 0:T1], scalar1=1.0 / D), inc="DVE")
            b.wait("vector", "DVE", vrb)
            vrc = b.op("vector", lambda e: e.tensor_mul(out=row_c, in0=row_a, in1=row_a), inc="DVE")
            b.wait("vector", "DVE", vrc)
            vvar = b.op("vector", lambda e: e.tensor_sub(out=row_b, in0=row_b, in1=row_c), inc="DVE")
            b.rel("pS", "DVE", vvar, 0, 2 * T1)
            b.rel("pY", "DVE", vvar, 0, T1)
            b.wait("scalar", "DVE", vvar)
            vln = b.op("scalar", lambda e: e.activation(out=row_b, in_=row_b, func=AF.Ln, bias=eps_sb[0:1, 0:1]), inc="ACT")
            b.wait("scalar", "ACT", vln)
            vrs = b.op("scalar", lambda e: e.activation(out=row_b, in_=row_b, func=AF.Exp, scale=-0.5), inc="ACT")
            b.wait("tensor", "ACT", vrs)
            b.use("tensor", "pC", 0, 2 * T1)
            b.op("tensor", lambda e: e.matmul(pC[:, 0:T1], ONES_ROW1, row_a, start=True, stop=True))
            vbc = b.op("tensor", lambda e: e.matmul(pC[:, T1:2 * T1], ONES_ROW1, row_b,
                                                    start=True, stop=True), inc="PE")
            b.wait("vector", "PE", vbc)
            b.op("vector", lambda e: e.tensor_copy(out=rs0b_sb, in_=pC[:, T1:2 * T1]))
            b.use("vector", "u2k")
            for g in range(8):
                usrc = u_sb[:, uofs + g * T1: uofs + (g + 1) * T1]
                dst = u2_sb[:, g * T1:(g + 1) * T1]
                b.op("vector", lambda e, usrc=usrc, dst=dst: e.tensor_sub(out=dst, in0=usrc, in1=pC[:, 0:T1]))
            vfu = b.mark("vector")
            b.wait("vector", "DVE", vfu)
            for g in range(8):
                dst = u2_sb[:, g * T1:(g + 1) * T1]
                b.op("vector", lambda e, dst=dst: e.tensor_mul(out=dst, in0=dst, in1=rs0b_sb))
            vu2 = b.mark("vector")
            b.rel("pC", "DVE", vu2, 0, 2 * T1)
            b.rel(f"ub{par}", "DVE", vu2)
            b.wait("tensor", "DVE", vu2)

            if it + 2 < nt1:
                b.use("sync", f"ub{par}")
                for g in range(8):
                    uval[it + 2] = b.dma("sync", u_sb[:, (par * 8 + g) * T1:(par * 8 + g + 1) * T1],
                                         uT[g * 128:(g + 1) * 128, (it + 2) * T1:(it + 3) * T1], f"LD{par}")
            if p1_stage < 2:
                continue

            mmr = [pA, pB]
            z0info = {}
            dt_pdst = dt_rb = dt_vmm = None
            for mi, (kind, gch, coff, mw) in enumerate(mplan):
                rb = mi % 2
                mmtag = ("pA", "pB")[rb]
                pdst = mmr[rb][0:mw, 0:T1]
                b.use("tensor", mmtag, 0, 512)
                for k in range(8):
                    b.op("tensor", lambda e, pdst=pdst, k=k, coff=coff, mw=mw: e.matmul(
                        pdst, win_sb[:, k * MUSE + coff:k * MUSE + coff + mw],
                        u2_sb[:, k * T1:(k + 1) * T1], start=(k == 0), stop=(k == 7)))
                vmm = b.mark("tensor")
                if kind == "z0":
                    zp = gch % 2
                    b.use("scalar", f"z0s{zp}")
                    b.wait("scalar", "PE", vmm)
                    vz = b.op("scalar", lambda e, pdst=pdst, zp=zp: e.activation(
                        out=z0s_sb[:, zp * T1:(zp + 1) * T1], in_=pdst, func=AF.Silu), inc="ACT")
                    b.rel(mmtag, "ACT", vz, 0, 512)
                    z0info[gch] = (zp, vz)
                elif kind == "x0":
                    zp, vz = z0info[gch]
                    mp = gch % 4
                    b.use("vector", f"mlp{mp}")
                    b.wait("vector", "PE", vmm)
                    b.wait("vector", "ACT", vz)
                    vml = b.op("vector", lambda e, pdst=pdst, zp=zp, mp=mp: e.tensor_mul(
                        out=mlp_sb[:, mp * T1:(mp + 1) * T1], in0=pdst,
                        in1=z0s_sb[:, zp * T1:(zp + 1) * T1]), inc="DVE")
                    b.rel(mmtag, "DVE", vml, 0, 512)
                    b.rel(f"z0s{zp}", "DVE", vml)
                    b.wait("gpsimd", "DVE", vml)
                    vsp = b.dma("gpsimd", mlp_sp[gch * 128:(gch + 1) * 128, t0:t0 + T1],
                                mlp_sb[:, mp * T1:(mp + 1) * T1], "ST")
                    b.rel(f"mlp{mp}", "ST", vsp)
                elif kind in ("X", "B", "C"):
                    if kind == "X":
                        dstb = cvx_sb[:, gch * (T1 + 3) + 3:(gch + 1) * (T1 + 3)]
                        key = f"cvx{gch}"
                    elif kind == "B":
                        dstb = cvb_sb[:, 3:T1 + 3]; key = "cvB"
                    else:
                        dstb = cvc_sb[:, 3:T1 + 3]; key = "cvC"
                    b.use("scalar", key)
                    b.wait("scalar", "PE", vmm)
                    vx = b.op("scalar", lambda e, pdst=pdst, dstb=dstb: e.copy(out=dstb, in_=pdst), inc="ACT")
                    b.rel(mmtag, "ACT", vx, 0, 512)
                    b.rel(f"evac_{key}", "ACT", vx)
                else:
                    dt_pdst, dt_rb, dt_vmm = pdst, rb, vmm
            b.rel("u2k", "PE", b.cnt["PE"])
            if p1_stage < 3:
                continue

            silu_done = {}
            for grp in ([0, 1, 2, 3], [4, 5, 6, 7], [8, 9, 10, 11], [12, 13, 14, 15], [16, 17]):
                info = {}
                for c in grp:
                    if c < 16:
                        buf = cvx_sb[:, c * (T1 + 3):(c + 1) * (T1 + 3)]
                        taps = [cwx_sb[:, 4 * c + k:4 * c + k + 1] for k in range(4)]
                        bias = cbx_sb[:, c:c + 1]; pr = 128
                        dst = xbf_sb[:, c * T1:(c + 1) * T1]
                        ekey = f"evac_cvx{c}"
                    elif c == 16:
                        buf = cvb_sb; taps = [cwb_sb[:, k:k + 1] for k in range(4)]
                        bias = cbb_sb; pr = 16; dst = bbf_sb; ekey = "evac_cvB"
                    else:
                        buf = cvc_sb; taps = [cwc_sb[:, k:k + 1] for k in range(4)]
                        bias = cbc_sb; pr = 16; dst = cbf_sb; ekey = "evac_cvC"
                    ap = c % 4
                    acc = cacc_sb[0:pr, ap * T1:(ap + 1) * T1]
                    info[c] = (buf, taps, bias, pr, dst, ekey, acc, ap)
                for c in grp:
                    buf, taps, bias, pr, dst, ekey, acc, ap = info[c]
                    b.use("vector", f"cacc{ap}")
                    b.use("vector", ekey)
                    b.op("vector", lambda e, buf=buf, acc=acc, t=taps[0], pr=pr: e.tensor_scalar_mul(
                        out=acc, in0=buf[0:pr, 0:T1], scalar1=t))
                for k in range(1, 4):
                    vf = b.mark("vector")
                    b.wait("vector", "DVE", vf)
                    for c in grp:
                        buf, taps, bias, pr, dst, ekey, acc, ap = info[c]
                        b.op("vector", lambda e, buf=buf, acc=acc, t=taps[k], k=k, pr=pr: e.scalar_tensor_tensor(
                            out=acc, in0=buf[0:pr, k:k + T1], scalar=t, in1=acc, op0=ALU.mult, op1=ALU.add))
                vcv = b.mark("vector")
                for c in grp:
                    b.rel(f"convdve{c}", "DVE", vcv)
                b.wait("scalar", "DVE", vcv)
                for c in grp:
                    buf, taps, bias, pr, dst, ekey, acc, ap = info[c]
                    b.use("scalar", f"xb{c}")
                    vsl = b.op("scalar", lambda e, acc=acc, dst=dst, bias=bias: e.activation(
                        out=dst, in_=acc, func=AF.Silu, bias=bias), inc="ACT")
                    b.rel(f"cacc{ap}", "ACT", vsl)
                    silu_done[c] = vsl

            for c in range(18):
                if c < 16:
                    buf = cvx_sb[:, c * (T1 + 3):(c + 1) * (T1 + 3)]; pr = 128; key = f"cvx{c}"
                elif c == 16:
                    buf = cvb_sb; pr = 16; key = "cvB"
                else:
                    buf = cvc_sb; pr = 16; key = "cvC"
                b.use("gpsimd", f"convdve{c}")
                vh = b.op("gpsimd", lambda e, buf=buf, pr=pr: e.tensor_copy(
                    out=buf[0:pr, 0:3], in_=buf[0:pr, T1:T1 + 3]), inc="POOL")
                b.rel(key, "POOL", vh)

            if p1_stage < 4:
                continue
            b.use("scalar", "dtk")
            b.wait("scalar", "PE", dt_vmm)
            vex0 = b.op("scalar", lambda e, p=dt_pdst: e.activation(out=dte_sb, in_=p, func=AF.Exp, bias=dtb_sb), inc="ACT")
            b.wait("scalar", "ACT", vex0)
            vdt = b.op("scalar", lambda e: e.activation(out=dt_sb, in_=dte_sb, func=AF.Ln, bias=1.0), inc="ACT")
            b.rel(("pA", "pB")[dt_rb], "ACT", vdt, 0, 512)
            b.wait("vector", "ACT", vdt)
            vddt = b.op("vector", lambda e: e.tensor_scalar_mul(out=ddt_sb, in0=dt_sb, scalar1=acol_sb), inc="DVE")

            for qq in range(T1 // Q):
                if p1_stage < 5:
                    break
                c0 = qq * Q
                cp = qq % 2
                b.use("vector", "s8")
                if qq == 0:
                    b.wait("vector", "DVE", vddt)
                vsc = b.op("vector", lambda e, c0=c0: e.tensor_tensor_scan(
                    out=s_sb, data0=ddt_sb[:, c0:c0 + Q], data1=zero8_sb, initial=0.0,
                    op0=ALU.add, op1=ALU.add), inc="DVE")
                b.wait("vector", "DVE", vsc)
                vs_ = b.op("vector", lambda e: e.tensor_scalar(
                    out=r_sb, in0=s_sb, scalar1=s_sb[:, Q - 1:Q], scalar2=-1.0,
                    op0=ALU.subtract, op1=ALU.mult), inc="DVE")
                b.wait("scalar", "DVE", vs_)
                b.use("scalar", "rows8")
                b.use("scalar", "ec8")
                b.op("scalar", lambda e: e.activation(out=exps_sb, in_=s_sb, func=AF.Exp))
                b.op("scalar", lambda e: e.activation(out=ens_sb, in_=s_sb, func=AF.Exp, scale=-1.0))
                b.op("scalar", lambda e: e.activation(out=er_sb, in_=r_sb, func=AF.Exp))
                vex = b.op("scalar", lambda e: e.activation(
                    out=ecol8_sb, in_=s_sb[:, Q - 1:Q], func=AF.Exp), inc="ACT")
                b.rel("s8", "ACT", vex)
                b.wait("vector", "ACT", vex)
                b.use("vector", "nq8")
                b.op("vector", lambda e, c0=c0: e.tensor_mul(out=nsdt_sb, in0=ens_sb, in1=dt_sb[:, c0:c0 + Q]))
                b.op("vector", lambda e, c0=c0: e.tensor_mul(out=qf_sb, in0=er_sb, in1=dt_sb[:, c0:c0 + Q]))
                b.use("vector", "diage")
                vrows = b.op("vector", lambda e: e.tensor_scalar_mul(
                    out=diage_sb, in0=cons_sb[0:NH, 0:NH], scalar1=ecol8_sb), inc="DVE")
                b.rel("ec8", "DVE", vrows)
                b.wait("tensor", "DVE", vrows)
                b.use("tensor", "pE", 0, 512)
                b.op("tensor", lambda e: e.transpose(pE[:, 128:136], nsdt_sb, IDE[0:NH, 0:NH]))
                b.op("tensor", lambda e: e.transpose(pE[:, 136:144], qf_sb, IDE[0:NH, 0:NH]))
                b.op("tensor", lambda e: e.transpose(pE[:, 144:152], exps_sb, IDE[0:NH, 0:NH]))
                b.op("tensor", lambda e: e.matmul(pE[0:16, 152:160], ONES8x16, diage_sb,
                                                  start=True, stop=True))
                b.wait("tensor", "ACT", silu_done[17])
                b.op("tensor", lambda e, c0=c0: e.matmul(
                    pE[:, 160:176], bbf_sb[:, c0:c0 + Q], IDE_BF[0:16, 0:16],
                    start=True, stop=True))
                vg = b.op("tensor", lambda e, c0=c0: e.matmul(
                    pE[:, 0:128], bbf_sb[:, c0:c0 + Q], cbf_sb[:, c0:c0 + Q],
                    start=True, stop=True), inc="PE")
                b.rel("rows8", "PE", vg)
                b.rel("nq8", "PE", vg)
                b.rel("diage", "PE", vg)
                b.wait("vector", "PE", vg)
                b.op("vector", lambda e: e.tensor_copy(out=tm_sb, in_=pE[:, 128:152]))
                b.op("vector", lambda e: e.tensor_copy(out=ecol16_sb, in_=pE[0:16, 152:160]))
                b.use("vector", f"btm{cp}")
                b.op("vector", lambda e, cp=cp: e.tensor_copy(
                    out=btm_sb[:, cp * 16:(cp + 1) * 16], in_=pE[:, 160:176]))
                b.use("vector", f"gm{cp}")
                vgm = b.op("vector", lambda e, cp=cp: e.tensor_mul(
                    out=gm_sb[:, cp * 128:(cp + 1) * 128], in0=pE[:, 0:128], in1=TRIU), inc="DVE")
                b.rel("pE", "DVE", vgm, 0, 176)

                xtmo = cp * 2048
                b.use("scalar", f"xtm{cp}")
                for cidx in range(16):
                    bank = pX1 if cidx < 8 else pX2
                    po = (cidx % 8) * 128
                    b.use("tensor", "pX1" if cidx < 8 else "pX2", po, po + 128)
                    b.wait("tensor", "ACT", silu_done[cidx])
                    b.op("tensor", lambda e, cidx=cidx, c0=c0, bank=bank, po=po: e.transpose(
                        bank[:, po:po + 128],
                        xbf_sb[:, cidx * T1 + c0:cidx * T1 + c0 + Q], IDE_BF))
                vtp = b.mark("tensor")
                b.wait("scalar", "PE", vtp)
                for pair in range(8):
                    bank = pX1 if pair < 4 else pX2
                    po = (pair % 4) * 256
                    vev = b.op("scalar", lambda e, pair=pair, bank=bank, po=po, xtmo=xtmo: e.copy(
                        out=xtm_sb[:, xtmo + pair * 256:xtmo + (pair + 1) * 256],
                        in_=bank[:, po:po + 256]), inc="ACT")
                vxtm = vev
                if debug:
                    b.wait("gpsimd", "ACT", vxtm)
                    b.dma("gpsimd", xtm_dbg[t0 + c0:t0 + c0 + Q, :], xtm_sb[:, xtmo:xtmo + 2048], "ST")
                b.rel("pX1", "ACT", vxtm, 0, 1024)
                b.rel("pX2", "ACT", vxtm, 0, 1024)
                b.wait("tensor", "ACT", vxtm)
                b.wait("vector", "ACT", vxtm)

                yslot = [(pY, 0), (pY, 256), (pA, 0), (pA, 256),
                         (pB, 0), (pB, 256), (pC, 0), (pC, 256)]
                vdl2 = {}
                vwps = {}
                for h in range(NH):
                    wslot = wp_sb[:, (cp * 8 + h) * 128:(cp * 8 + h + 1) * 128]
                    b.use("vector", f"wp{cp}{h}")
                    if h == 0:
                        b.wait("vector", "DVE", vgm)
                    vwps[h] = b.op("vector", lambda e, wslot=wslot, cp=cp, h=h: e.tensor_scalar_mul(
                        out=wslot, in0=gm_sb[:, cp * 128:(cp + 1) * 128],
                        scalar1=tm_sb[:, h:h + 1]), inc="DVE")
                for h in range(NH):
                    xtm_h = xtm_sb[:, xtmo + h * 256:xtmo + (h + 1) * 256]
                    wslot = wp_sb[:, (cp * 8 + h) * 128:(cp * 8 + h + 1) * 128]
                    vwp = vwps[h]
                    b.wait("tensor", "DVE", vwp)
                    ybank, yo = yslot[h]
                    ytag = {id(pY): "pY", id(pA): "pA", id(pB): "pB", id(pC): "pC"}[id(ybank)]
                    b.use("tensor", ytag, yo, yo + 256)
                    b.op("tensor", lambda e, ybank=ybank, yo=yo, wslot=wslot, xtm_h=xtm_h: e.matmul(
                        ybank[:, yo:yo + 256], wslot, xtm_h, start=True, stop=False))
                    vy = b.op("tensor", lambda e, ybank=ybank, yo=yo, h=h, c0=c0: e.matmul(
                        ybank[:, yo:yo + 256], cbf_sb[:, c0:c0 + Q],
                        hbf_sb[:, h * 256:(h + 1) * 256], start=False, stop=True), inc="PE")
                    b.rel(f"wp{cp}{h}", "PE", vy)
                    b.wait("vector", "PE", vy)
                    b.use("vector", f"y1{h % 2}")
                    vy1 = b.op("vector", lambda e, ybank=ybank, yo=yo, h=h: e.tensor_scalar_mul(
                        out=y1_sb[:, (h % 2) * 256:(h % 2 + 1) * 256], in0=ybank[:, yo:yo + 256],
                        scalar1=tm_sb[:, 16 + h:17 + h]), inc="DVE")
                    xqs = xq_sb[:, (h % 4) * 256:(h % 4 + 1) * 256]
                    b.use("vector", f"xq{(h // 2) % 2}") if h % 2 == 0 else None
                    b.op("vector", lambda e, xqs=xqs, h=h, xtm_h=xtm_h: e.tensor_scalar_mul(
                        out=xqs, in0=xtm_h, scalar1=tm_sb[:, 8 + h:9 + h]))
                    b.wait("vector", "DVE", vy1)
                    vyt = b.op("vector", lambda e, h=h, xtm_h=xtm_h: e.scalar_tensor_tensor(
                        out=ytm_sb[:, h * 256:(h + 1) * 256], in0=xtm_h, scalar=dtm_sb[:, h:h + 1],
                        in1=y1_sb[:, (h % 2) * 256:(h % 2 + 1) * 256],
                        op0=ALU.mult, op1=ALU.add), inc="DVE")
                    if debug and it == 0 and qq == 0 and h == 0:
                        vypc = b.op("vector", lambda e, ybank=ybank, yo=yo: e.tensor_copy(
                            out=cacc_sb[:, 0:256], in_=ybank[:, yo:yo + 256]), inc="DVE")
                        b.wait("gpsimd", "DVE", vypc)
                        b.dma("gpsimd", yp_dbg, cacc_sb[:, 0:256], "ST")
                        b.dma("gpsimd", y1_dbg, y1_sb[:, 0:256], "ST")
                    b.rel(ytag, "DVE", vyt, yo, yo + 256)
                    b.rel(f"y1{h % 2}", "DVE", vyt)
                    b.wait("tensor", "DVE", vyt)
                    for j in range(2):
                        cidx = 2 * h + j
                        bank = pX1 if cidx < 8 else pX2
                        po = (cidx % 8) * 128
                        b.use("tensor", "pX1" if cidx < 8 else "pX2", po, po + 128)
                        vbtp = b.op("tensor", lambda e, h=h, j=j, bank=bank, po=po: e.transpose(
                            bank[:, po:po + 128],
                            ytm_sb[:, h * 256 + j * 128:h * 256 + j * 128 + 128],
                            IDE_BF), inc="PE")
                        b.wait("scalar", "PE", vbtp)
                        b.use("scalar", "yfmk")
                        vyf = b.op("scalar", lambda e, cidx=cidx, c0=c0, bank=bank, po=po: e.copy(
                            out=yfm_sb[:, cidx * T1 + c0:cidx * T1 + c0 + Q],
                            in_=bank[:, po:po + 128]), inc="ACT")
                        if debug and it == 0 and qq == 0 and h == 0:
                            b.wait("gpsimd", "ACT", vyf)
                            b.dma("gpsimd", px_dbg[:, j * 128:(j + 1) * 128],
                                  yfm_sb[:, cidx * T1 + c0:cidx * T1 + c0 + Q], "ST")
                        b.rel("pX1" if cidx < 8 else "pX2", "ACT", vyf, po, po + 128)
                    if h % 2 == 1:
                        vxq = b.mark("vector")
                        b.wait("tensor", "DVE", vxq)
                        dslot = [(pE, 0), (pS, 0), (pE, 0), (pS, 0)][(h // 2)]
                        dbank, do = dslot
                        dtag = "pE" if dbank is pE else "pS"
                        b.use("tensor", dtag, do, do + 512)
                        vdl = b.op("tensor", lambda e, dbank=dbank, do=do, cp=cp, h=h: e.matmul(
                            dbank[0:16, do:do + 512], btm_sb[:, cp * 16:(cp + 1) * 16],
                            xq_sb[:, ((h - 1) % 4) * 256:((h - 1) % 4 + 2) * 256],
                            start=True, stop=True), inc="PE")
                        b.wait("vector", "PE", vdl)
                        for hh in (h - 1, h):
                            vhn = b.op("vector", lambda e, hh=hh, dbank=dbank, do=do, h=h: e.scalar_tensor_tensor(
                                out=h_sb[:, hh * 256:(hh + 1) * 256],
                                in0=h_sb[:, hh * 256:(hh + 1) * 256],
                                scalar=ecol16_sb[:, hh:hh + 1],
                                in1=dbank[0:16, do + (hh - (h - 1)) * 256:do + (hh - (h - 1)) * 256 + 256],
                                op0=ALU.mult, op1=ALU.add), inc="DVE")
                        b.rel(dtag, "DVE", vhn, do, do + 512)
                        b.rel(f"xq{(h // 2) % 2}", "PE", vdl)
                        b.wait("scalar", "DVE", vhn)
                        for hh in (h - 1, h):
                            vhb = b.op("scalar", lambda e, hh=hh: e.copy(
                                out=hbf_sb[:, hh * 256:(hh + 1) * 256],
                                in_=h_sb[:, hh * 256:(hh + 1) * 256]), inc="ACT")
                b.wait("tensor", "ACT", vhb)
                if debug:
                    vdd = b.mark("vector")
                    b.wait("gpsimd", "DVE", vdd)
                    b.dma("gpsimd", ytm_dbg[t0 + c0:t0 + c0 + Q, :], ytm_sb[:, 0:2048], "ST")
                    if it == 0:
                        b.wait("gpsimd", "ACT", b.cnt["ACT"])
                        b.dma("gpsimd", (yf1_dbg if qq == 0 else yf2_dbg)[:, 0:128],
                              yfm_sb[:, c0:c0 + 128], "ST")
                b.rel(f"xtm{cp}", "PE", b.cnt["PE"])
                b.rel(f"xtm{cp}", "DVE", b.cnt["DVE"])
                b.rel(f"btm{cp}", "PE", b.cnt["PE"])
                b.rel(f"gm{cp}", "DVE", b.cnt["DVE"])
            b.rel("dtk", "DVE", b.cnt["DVE"])
            for c in range(18):
                b.rel(f"xb{c}", "PE", b.cnt["PE"])

            if p1_stage < 6:
                continue
            import os as _os2
            vyfm = b.cnt["ACT"]
            if debug:
                b.wait("gpsimd", "ACT", vyfm)
                if it == 0:
                    b.dma("gpsimd", yf3_dbg, yfm_sb[:, 0:256], "ST")
                for g in range(16):
                    b.dma("gpsimd", y_dbg[g * 128:(g + 1) * 128, t0:t0 + T1],
                          yfm_sb[:, g * T1:(g + 1) * T1], "ST")
                    b.dma("gpsimd", xbf_dbg[g * 128:(g + 1) * 128, t0:t0 + T1],
                          xbf_sb[:, g * T1:(g + 1) * T1], "ST")
            b.use("tensor", "pS", 0, 2 * T1)
            b.use("tensor", "pY", 0, T1)
            b.wait("tensor", "ACT", vyfm)
            b.wait("scalar", "ACT", vyfm)
            b.use("scalar", "yfmk")
            pend_y = None
            ng_y = 16 if not _os2.environ.get("NOSTATS") else 0
            for g in range(ng_y):
                ych = yfm_sb[:, g * T1:(g + 1) * T1]
                b.op("tensor", lambda e, ych=ych, g=g: e.matmul(
                    pS[0:1, 0:T1], ONES_COL_BF, ych, start=(g == 0), stop=(g == 15)))
                sq = g % 2
                b.use("scalar", f"ysq{sq}")
                vsq = b.op("scalar", lambda e, ych=ych, sq=sq: e.activation(
                    out=ysq_sb[:, sq * T1:(sq + 1) * T1], in_=ych, func=AF.Square), inc="ACT")
                for pg, pvsq in ([pend_y] if pend_y else []) + ([(g, vsq)] if g == ng_y - 1 else []):
                    psq = pg % 2
                    b.wait("tensor", "ACT", pvsq)
                    vqm = b.op("tensor", lambda e, psq=psq, pg=pg: e.matmul(
                        pY[0:1, 0:T1], ONES_COL_BF, ysq_sb[:, psq * T1:(psq + 1) * T1],
                        start=(pg == 0), stop=(pg == 15)), inc="PE")
                    b.rel(f"ysq{psq}", "PE", vqm)
                pend_y = (g, vsq)
            vyst = b.mark("tensor")
            if debug and it == 0:
                b.wait("gpsimd", "PE", vyst)
                b.dma("gpsimd", yf4_dbg, yfm_sb[:, 0:256], "ST")
            b.wait("vector", "PE", vyst)
            b.op("vector", lambda e: e.tensor_scalar_mul(out=row_a, in0=pS[0:1, 0:T1], scalar1=1.0 / DIN))
            vrb2 = b.op("vector", lambda e: e.tensor_scalar_mul(out=row_b, in0=pY[0:1, 0:T1], scalar1=1.0 / DIN), inc="DVE")
            b.wait("vector", "DVE", vrb2)
            vrc2 = b.op("vector", lambda e: e.tensor_mul(out=row_c, in0=row_a, in1=row_a), inc="DVE")
            b.wait("vector", "DVE", vrc2)
            vv2 = b.op("vector", lambda e: e.tensor_sub(out=row_b, in0=row_b, in1=row_c), inc="DVE")
            b.rel("pS", "DVE", vv2, 0, 2 * T1)
            b.rel("pY", "DVE", vv2, 0, T1)
            b.wait("scalar", "DVE", vv2)
            vln2 = b.op("scalar", lambda e: e.activation(out=row_b, in_=row_b, func=AF.Ln, bias=eps_sb[0:1, 0:1]), inc="ACT")
            b.wait("scalar", "ACT", vln2)
            vrsy = b.op("scalar", lambda e: e.activation(out=row_b, in_=row_b, func=AF.Exp, scale=-0.5), inc="ACT")
            b.wait("vector", "ACT", vrsy)
            vmr = b.op("vector", lambda e: e.scalar_tensor_tensor(
                out=row_c, in0=row_a, scalar=-1.0, in1=row_b, op0=ALU.mult, op1=ALU.mult), inc="DVE")
            b.wait("tensor", "DVE", vmr)
            b.use("tensor", "pC", 0, 2 * T1)
            b.op("tensor", lambda e: e.matmul(pC[:, 0:T1], ONES_ROW1, row_b, start=True, stop=True))
            vbc2 = b.op("tensor", lambda e: e.matmul(pC[:, T1:2 * T1], ONES_ROW1, row_c,
                                                     start=True, stop=True), inc="PE")
            b.wait("vector", "PE", vbc2)
            b.op("vector", lambda e: e.tensor_copy(out=ryb_sb, in_=pC[:, 0:T1]))
            vmrb = b.op("vector", lambda e: e.tensor_copy(out=mrb_sb, in_=pC[:, T1:2 * T1]), inc="DVE")
            b.rel("pC", "DVE", vmrb, 0, 2 * T1)
            b.wait("vector", "DVE", vmrb)
            for gg in range(0, 16 if not _os2.environ.get("NOYLN") else 0, 4):
                for g in range(gg, gg + 4):
                    lp = g % 4
                    ych = yfm_sb[:, g * T1:(g + 1) * T1]
                    dst = yln_sb[:, lp * T1:(lp + 1) * T1]
                    b.use("vector", f"yln{lp}")
                    b.op("vector", lambda e, ych=ych, dst=dst: e.tensor_mul(out=dst, in0=ych, in1=ryb_sb))
                vf1 = b.mark("vector")
                b.wait("vector", "DVE", vf1)
                for g in range(gg, gg + 4):
                    dst = yln_sb[:, (g % 4) * T1:(g % 4 + 1) * T1]
                    b.op("vector", lambda e, dst=dst: e.tensor_add(out=dst, in0=dst, in1=mrb_sb))
                vf2 = b.mark("vector")
                b.wait("vector", "DVE", vf2)
                for g in range(gg, gg + 4):
                    dst = yln_sb[:, (g % 4) * T1:(g % 4 + 1) * T1]
                    vy3 = b.op("vector", lambda e, dst=dst, g=g: e.tensor_scalar(
                        out=dst, in0=dst, scalar1=lng_sb[:, g:g + 1], scalar2=lnbt_sb[:, g:g + 1],
                        op0=ALU.mult, op1=ALU.add), inc="DVE")
                    b.wait("gpsimd", "DVE", vy3)
                    vsp = b.dma("gpsimd", yln_sp[g * 128:(g + 1) * 128, t0:t0 + T1], dst, "ST")
                    b.rel(f"yln{g % 4}", "ST", vsp)
            if debug and it == 0:
                b.wait("gpsimd", "DVE", b.cnt["DVE"])
                b.dma("gpsimd", yf5_dbg, yfm_sb[:, 0:256], "ST")
            b.rel("yfmk", "DVE", b.cnt["DVE"])
            b.rel("yfmk", "PE", vyst)

        b.barrier()
        b.flush()

    pws.close()
    if "a" not in phases:
        b.wait("sync", "ST", b.cnt["ST"])
        b.flush()
        ctx.close()
        return nc
    with ExitStack() as p2s:
        A = lambda n, shp, dt=F32: p2s.enter_context(nc.sbuf_tensor(n, list(shp), dt)).ap()
        wg_sb = A("wg_sb", [128, 16 * DIN], BF16)
        yl_sb = A("yl_sb", [128, 2 * 16 * T2], BF16)
        gs_sb = A("gs_sb", [128, 2 * T2], BF16)
        yg_sb = A("yg_sb", [128, 2 * T2], BF16)

        vwg = None
        for k in range(16):
            vwg = b.dma("sync", wg_sb[:, k * DIN:(k + 1) * DIN], wg_bf[k * 128:(k + 1) * 128, :], "LDP")

        def issue_yl(it2):
            pp = it2 % 2
            b.use("sync", f"ylt{pp}")
            v = None
            for g in range(16):
                v = b.dma("sync", yl_sb[:, (pp * 16 + g) * T2:(pp * 16 + g + 1) * T2],
                          yln_sp[g * 128:(g + 1) * 128, it2 * T2:(it2 + 1) * T2], f"LT{pp}")
            return v

        vyl_t = {0: issue_yl(0)}
        for it in range(nt2):
            t0 = it * T2
            pp = it % 2
            if it + 1 < nt2:
                vyl_t[it + 1] = issue_yl(it + 1)
            ylo = pp * 16 * T2
            b.wait("tensor", "LDP", vwg)
            b.wait("tensor", f"LT{pp}", vyl_t[it])
            b.wait("vector", f"LT{pp}", vyl_t[it])
            for m in range(16):
                rb = m % 2
                pdst = [pA, pB][rb]
                b.use("tensor", f"mm{rb}")
                for k in range(16):
                    b.op("tensor", lambda e, pdst=pdst, k=k, m=m, ylo=ylo: e.matmul(
                        pdst[:, 0:T2], wg_sb[:, k * DIN + m * 128:k * DIN + (m + 1) * 128],
                        yl_sb[:, ylo + k * T2:ylo + (k + 1) * T2], start=(k == 0), stop=(k == 15)))
                vmm = b.mark("tensor")
                b.wait("scalar", "PE", vmm)
                b.use("scalar", f"gs{rb}")
                vsg = b.op("scalar", lambda e, pdst=pdst, rb=rb, m=m: e.activation(
                    out=gs_sb[:, rb * T2:(rb + 1) * T2], in_=pdst[:, 0:T2], func=AF.Sigmoid,
                    bias=bgat_sb[:, m:m + 1]), inc="ACT")
                b.rel(f"mm{rb}", "ACT", vsg)
                b.wait("vector", "ACT", vsg)
                b.use("vector", f"yg{rb}")
                vyg = b.op("vector", lambda e, rb=rb, m=m, ylo=ylo: e.tensor_mul(
                    out=yg_sb[:, rb * T2:(rb + 1) * T2], in0=gs_sb[:, rb * T2:(rb + 1) * T2],
                    in1=yl_sb[:, ylo + m * T2:ylo + (m + 1) * T2]), inc="DVE")
                b.rel(f"gs{rb}", "DVE", vyg)
                b.wait("gpsimd", "DVE", vyg)
                vst = b.dma("gpsimd", ygat_sp[m * 128:(m + 1) * 128, t0:t0 + T2],
                            yg_sb[:, rb * T2:(rb + 1) * T2], "ST")
                b.rel(f"yg{rb}", "ST", vst)
            b.rel(f"ylt{pp}", "DVE", b.cnt["DVE"])
        b.barrier()
        b.flush()

    if "b" not in phases:
        b.wait("sync", "ST", b.cnt["ST"])
        b.flush()
        ctx.close()
        return nc
    with ExitStack() as p2bs:
        A = lambda n, shp, dt=F32: p2bs.enter_context(nc.sbuf_tensor(n, list(shp), dt)).ap()
        wo_sb = A("wo_sb", [128, 32 * D], BF16)
        ml_sb = A("ml_sb", [128, 2 * 16 * T2], BF16)
        yg2_sb = A("yg2_sb", [128, 2 * 16 * T2], BF16)
        u3_sb = A("u3_sb", [128, 2 * 8 * T2])
        x1_sb = A("x1_sb", [128, 2 * T2])
        x1q_sb = A("x1q_sb", [128, 2 * T2], BF16)
        srow_sb = A("srow_sb", [1, 2 * T2])

        vwo = None
        for k in range(32):
            vwo = b.dma("sync", wo_sb[:, k * D:(k + 1) * D], wo_bf[k * 128:(k + 1) * 128, :], "LDP")

        def issue_p2b(it2):
            pp = it2 % 2
            b.use("sync", f"p2bt{pp}")
            tt = it2 * T2
            v = None
            for g in range(16):
                v = b.dma("sync", ml_sb[:, (pp * 16 + g) * T2:(pp * 16 + g + 1) * T2],
                          mlp_sp[g * 128:(g + 1) * 128, tt:tt + T2], f"LT{pp}")
                v = b.dma("sync", yg2_sb[:, (pp * 16 + g) * T2:(pp * 16 + g + 1) * T2],
                          ygat_sp[g * 128:(g + 1) * 128, tt:tt + T2], f"LT{pp}")
            for g in range(8):
                v = b.dma("sync", u3_sb[:, (pp * 8 + g) * T2:(pp * 8 + g + 1) * T2],
                          uT[g * 128:(g + 1) * 128, tt:tt + T2], f"LT{pp}")
            return v

        vld_t = {0: issue_p2b(0)}
        for it in range(nt2):
            t0 = it * T2
            pp = it % 2
            if it + 1 < nt2:
                vld_t[it + 1] = issue_p2b(it + 1)
            mlo = pp * 16 * T2
            uo = pp * 8 * T2
            b.wait("tensor", "LDP", vwo)
            b.wait("tensor", f"LT{pp}", vld_t[it])
            b.wait("vector", f"LT{pp}", vld_t[it])
            b.use("tensor", "pS", 0, T2)
            b.use("tensor", "pY", 0, T2)
            pend_b = None
            for m in range(8):
                rb = m % 2
                pdst = [pA, pB][rb]
                b.use("tensor", f"mm{rb}")
                for k in range(32):
                    rhs = (ml_sb[:, mlo + k * T2:mlo + (k + 1) * T2] if k < 16
                           else yg2_sb[:, mlo + (k - 16) * T2:mlo + (k - 15) * T2])
                    b.op("tensor", lambda e, pdst=pdst, k=k, m=m, rhs=rhs: e.matmul(
                        pdst[:, 0:T2], wo_sb[:, k * D + m * 128:k * D + (m + 1) * 128], rhs,
                        start=(k == 0), stop=(k == 31)))
                vmm = b.mark("tensor")
                b.wait("vector", "PE", vmm)
                b.use("vector", f"x1{rb}")
                vx1 = b.op("vector", lambda e, pdst=pdst, rb=rb, m=m, uo=uo: e.tensor_add(
                    out=x1_sb[:, rb * T2:(rb + 1) * T2], in0=pdst[:, 0:T2],
                    in1=u3_sb[:, uo + m * T2:uo + (m + 1) * T2]), inc="DVE")
                b.rel(f"mm{rb}", "DVE", vx1)
                b.wait("gpsimd", "DVE", vx1)
                vst = b.dma("gpsimd", x1_sp[m * 128:(m + 1) * 128, t0:t0 + T2],
                            x1_sb[:, rb * T2:(rb + 1) * T2], "ST")
                b.wait("scalar", "DVE", vx1)
                b.use("scalar", f"x1q{rb}")
                vsq = b.op("scalar", lambda e, rb=rb: e.activation(
                    out=x1q_sb[:, rb * T2:(rb + 1) * T2], in_=x1_sb[:, rb * T2:(rb + 1) * T2],
                    func=AF.Square), inc="ACT")
                b.rel(f"x1{rb}", "ST", vst)
                for pm, pvx1, pvsq in ([pend_b] if pend_b else []) + ([(m, vx1, vsq)] if m == 7 else []):
                    prb = pm % 2
                    b.wait("tensor", "DVE", pvx1)
                    b.op("tensor", lambda e, prb=prb, pm=pm: e.matmul(
                        pS[0:1, 0:T2], ONES_COL, x1_sb[:, prb * T2:(prb + 1) * T2],
                        start=(pm == 0), stop=(pm == 7)))
                    b.wait("tensor", "ACT", pvsq)
                    vqm = b.op("tensor", lambda e, prb=prb, pm=pm: e.matmul(
                        pY[0:1, 0:T2], ONES_COL_BF, x1q_sb[:, prb * T2:(prb + 1) * T2],
                        start=(pm == 0), stop=(pm == 7)), inc="PE")
                    b.rel(f"x1q{prb}", "PE", vqm)
                pend_b = (m, vx1, vsq)
            vst8 = b.mark("tensor")
            b.wait("vector", "PE", vst8)
            b.use("vector", "srow")
            b.op("vector", lambda e: e.tensor_copy(out=srow_sb[0:1, 0:T2], in_=pS[0:1, 0:T2]))
            vsr = b.op("vector", lambda e: e.tensor_copy(out=srow_sb[0:1, T2:2 * T2], in_=pY[0:1, 0:T2]), inc="DVE")
            b.rel("pS", "DVE", vsr, 0, T2)
            b.rel("pY", "DVE", vsr, 0, T2)
            b.wait("gpsimd", "DVE", vsr)
            b.dma("gpsimd", st_sp[0:1, t0:t0 + T2], srow_sb[0:1, 0:T2], "ST")
            vst2 = b.dma("gpsimd", st_sp[1:2, t0:t0 + T2], srow_sb[0:1, T2:2 * T2], "ST")
            b.rel("srow", "ST", vst2)
            b.rel(f"p2bt{pp}", "PE", vst8)
            b.rel(f"p2bt{pp}", "DVE", vsr)
        b.barrier()
        b.flush()

    if "3" not in phases:
        b.wait("sync", "ST", b.cnt["ST"])
        b.flush()
        ctx.close()
        return nc
    with ExitStack() as p3s:
        A = lambda n, shp, dt=F32: p3s.enter_context(nc.sbuf_tensor(n, list(shp), dt)).ap()
        wfc_sb = A("wfc_sb", [128, 8 * H], BF16)
        wpr_sb = A("wpr_sb", [128, 32 * D], BF16)
        x14_sb = A("x14_sb", [128, 8 * T2])
        x1p_sb = A("x1p_sb", [128, 8 * T2], BF16)
        hr_sb = A("hr_sb", [128, 32 * T2], BF16)
        hsq_sb = A("hsq_sb", [128, 2 * T2], BF16)
        ot_sb = A("ot_sb", [128, 2 * T2])
        bcb_sb = A("bcb_sb", [128, 2 * T2], BF16)
        rw1_sb = A("rw1_sb", [1, T2])
        rw2_sb = A("rw2_sb", [1, T2])
        rw3_sb = A("rw3_sb", [1, T2])

        vw = None
        for k in range(8):
            vw = b.dma("sync", wfc_sb[:, k * H:(k + 1) * H], wfcq[k * 128:(k + 1) * 128, :], "LDP")
        for k in range(32):
            vw = b.dma("sync", wpr_sb[:, k * D:(k + 1) * D], wprq[k * 128:(k + 1) * 128, :], "LDP")
        for it in range(nt2):
            t0 = it * T2
            b.use("sync", "p3t")
            vld = None
            for g in range(8):
                vld = b.dma("sync", x14_sb[:, g * T2:(g + 1) * T2],
                            x1_sp[g * 128:(g + 1) * 128, t0:t0 + T2], "LDT")
            vr1 = b.dma("sync", rw1_sb, st_sp[0:1, t0:t0 + T2], "LDT")
            vr2 = b.dma("sync", rw2_sb, st_sp[1:2, t0:t0 + T2], "LDT")
            b.wait("vector", "LDT", vr2)
            b.wait("tensor", "LDP", vw)
            b.op("vector", lambda e: e.tensor_scalar_mul(out=rw1_sb, in0=rw1_sb, scalar1=1.0 / D))
            vwa = b.op("vector", lambda e: e.tensor_scalar_mul(out=rw2_sb, in0=rw2_sb, scalar1=1.0 / D), inc="DVE")
            b.wait("vector", "DVE", vwa)
            vwb = b.op("vector", lambda e: e.tensor_mul(out=rw3_sb, in0=rw1_sb, in1=rw1_sb), inc="DVE")
            b.wait("vector", "DVE", vwb)
            vv = b.op("vector", lambda e: e.tensor_sub(out=rw2_sb, in0=rw2_sb, in1=rw3_sb), inc="DVE")
            b.wait("scalar", "DVE", vv)
            vwl = b.op("scalar", lambda e: e.activation(out=rw2_sb, in_=rw2_sb, func=AF.Ln, bias=eps_sb[0:1, 0:1]), inc="ACT")
            b.wait("scalar", "ACT", vwl)
            vrs = b.op("scalar", lambda e: e.activation(out=rw2_sb, in_=rw2_sb, func=AF.Exp, scale=-0.5), inc="ACT")
            b.wait("vector", "ACT", vrs)
            vrh = b.op("vector", lambda e: e.tensor_scalar_mul(out=rw2_sb, in0=rw2_sb, scalar1=0.5), inc="DVE")
            b.wait("tensor", "DVE", vrh)
            b.use("tensor", "pC", 0, T2)
            b.use("tensor", "pE", 0, T2)
            b.op("tensor", lambda e: e.matmul(pC[:, 0:T2], ONES_ROW1, rw1_sb, start=True, stop=True))
            vbc = b.op("tensor", lambda e: e.matmul(pE[:, 0:T2], ONES_ROW1, rw2_sb,
                                                    start=True, stop=True), inc="PE")
            b.wait("vector", "PE", vbc)
            b.use("vector", "bcb")
            vrb = b.op("vector", lambda e: e.tensor_copy(out=bcb_sb[:, 0:T2], in_=pE[:, 0:T2]), inc="DVE")
            b.rel("pE", "DVE", vrb, 0, T2)
            b.wait("vector", "DVE", vrb)
            b.use("vector", "x1pall")
            for g in range(8):
                dst = x1p_sb[:, g * T2:(g + 1) * T2]
                b.op("vector", lambda e, g=g, dst=dst: e.tensor_sub(
                    out=dst, in0=x14_sb[:, g * T2:(g + 1) * T2], in1=pC[:, 0:T2]))
            vfx = b.mark("vector")
            b.wait("vector", "DVE", vfx)
            for g in range(8):
                dst = x1p_sb[:, g * T2:(g + 1) * T2]
                b.op("vector", lambda e, dst=dst: e.tensor_mul(out=dst, in0=dst, in1=bcb_sb[:, 0:T2]))
            vx1p = b.mark("vector")
            b.rel("pC", "DVE", vx1p, 0, T2)
            b.rel("bcb", "DVE", vx1p)
            b.wait("tensor", "DVE", vx1p)
            b.use("tensor", "pS", 0, T2)
            b.use("tensor", "pY", 0, T2)
            b.use("scalar", "hrall")
            pend_h = None
            for m in range(32):
                rb = m % 2
                pdst = [pA, pB][rb]
                b.use("tensor", f"mm{rb}")
                for k in range(8):
                    b.op("tensor", lambda e, pdst=pdst, k=k, m=m: e.matmul(
                        pdst[:, 0:T2], wfc_sb[:, k * H:(k + 1) * H][:, m * 128:(m + 1) * 128],
                        x1p_sb[:, k * T2:(k + 1) * T2], start=(k == 0), stop=(k == 7)))
                vmm = b.mark("tensor")
                b.wait("scalar", "PE", vmm)
                vgl = b.op("scalar", lambda e, pdst=pdst, m=m: e.activation(
                    out=hr_sb[:, m * T2:(m + 1) * T2], in_=pdst[:, 0:T2], func=AF.Gelu), inc="ACT")
                b.rel(f"mm{rb}", "ACT", vgl)
                sq = m % 2
                b.use("scalar", f"hsq{sq}")
                b.wait("scalar", "ACT", vgl)
                vsq = b.op("scalar", lambda e, m=m, sq=sq: e.activation(
                    out=hsq_sb[:, sq * T2:(sq + 1) * T2], in_=hr_sb[:, m * T2:(m + 1) * T2],
                    func=AF.Square), inc="ACT")
                for pm, pvgl, pvsq in ([pend_h] if pend_h else []) + ([(m, vgl, vsq)] if m == 31 else []):
                    psq = pm % 2
                    b.wait("tensor", "ACT", pvgl)
                    b.op("tensor", lambda e, pm=pm: e.matmul(
                        pS[0:1, 0:T2], ONES_COL_BF, hr_sb[:, pm * T2:(pm + 1) * T2],
                        start=(pm == 0), stop=(pm == 31)))
                    b.wait("tensor", "ACT", pvsq)
                    vqm = b.op("tensor", lambda e, psq=psq, pm=pm: e.matmul(
                        pY[0:1, 0:T2], ONES_COL_BF, hsq_sb[:, psq * T2:(psq + 1) * T2],
                        start=(pm == 0), stop=(pm == 31)), inc="PE")
                    b.rel(f"hsq{psq}", "PE", vqm)
                pend_h = (m, vgl, vsq)
            vst3 = b.mark("tensor")
            b.wait("vector", "PE", vst3)
            b.op("vector", lambda e: e.tensor_scalar_mul(out=rw1_sb, in0=pS[0:1, 0:T2], scalar1=1.0 / H))
            vwa3 = b.op("vector", lambda e: e.tensor_scalar_mul(out=rw2_sb, in0=pY[0:1, 0:T2], scalar1=1.0 / H), inc="DVE")
            b.wait("vector", "DVE", vwa3)
            vwb3 = b.op("vector", lambda e: e.tensor_mul(out=rw3_sb, in0=rw1_sb, in1=rw1_sb), inc="DVE")
            b.wait("vector", "DVE", vwb3)
            vv3 = b.op("vector", lambda e: e.tensor_sub(out=rw2_sb, in0=rw2_sb, in1=rw3_sb), inc="DVE")
            b.rel("pS", "DVE", vv3, 0, T2)
            b.rel("pY", "DVE", vv3, 0, T2)
            b.wait("scalar", "DVE", vv3)
            vwl3 = b.op("scalar", lambda e: e.activation(out=rw2_sb, in_=rw2_sb, func=AF.Ln, bias=eps_sb[0:1, 0:1]), inc="ACT")
            b.wait("scalar", "ACT", vwl3)
            vrs3 = b.op("scalar", lambda e: e.activation(out=rw2_sb, in_=rw2_sb, func=AF.Exp, scale=-0.5), inc="ACT")
            b.wait("vector", "ACT", vrs3)
            vrh3 = b.op("vector", lambda e: e.tensor_scalar_mul(out=rw2_sb, in0=rw2_sb, scalar1=0.5), inc="DVE")
            b.wait("tensor", "DVE", vrh3)
            b.use("tensor", "pC", 0, T2)
            b.use("tensor", "pE", 0, T2)
            b.op("tensor", lambda e: e.matmul(pC[:, 0:T2], ONES_ROW1, rw1_sb, start=True, stop=True))
            vbc3 = b.op("tensor", lambda e: e.matmul(pE[:, 0:T2], ONES_ROW1, rw2_sb,
                                                     start=True, stop=True), inc="PE")
            b.wait("vector", "PE", vbc3)
            b.use("vector", "bcb")
            b.op("vector", lambda e: e.tensor_copy(out=bcb_sb[:, 0:T2], in_=pC[:, 0:T2]))
            vmu3 = b.op("vector", lambda e: e.tensor_copy(out=bcb_sb[:, T2:2 * T2], in_=pE[:, 0:T2]), inc="DVE")
            b.rel("pC", "DVE", vmu3, 0, T2)
            b.rel("pE", "DVE", vmu3, 0, T2)
            b.wait("vector", "DVE", vmu3)
            for m in range(32):
                hch = hr_sb[:, m * T2:(m + 1) * T2]
                b.op("vector", lambda e, hch=hch: e.tensor_sub(out=hch, in0=hch, in1=bcb_sb[:, 0:T2]))
            vfh = b.mark("vector")
            b.wait("vector", "DVE", vfh)
            for m in range(32):
                hch = hr_sb[:, m * T2:(m + 1) * T2]
                b.op("vector", lambda e, hch=hch, m=m: e.scalar_tensor_tensor(
                    out=hch, in0=hch, scalar=flnw_sb[:, m:m + 1], in1=bcb_sb[:, T2:2 * T2],
                    op0=ALU.mult, op1=ALU.mult))
            vh2 = b.mark("vector")
            b.rel("bcb", "DVE", vh2)
            b.wait("tensor", "DVE", vh2)
            for m in range(8):
                rb = m % 2
                pdst = [pA, pB][rb]
                b.use("tensor", f"mm{rb}")
                for k in range(32):
                    b.op("tensor", lambda e, pdst=pdst, k=k, m=m: e.matmul(
                        pdst[:, 0:T2], wpr_sb[:, k * D:(k + 1) * D][:, m * 128:(m + 1) * 128],
                        hr_sb[:, k * T2:(k + 1) * T2], start=(k == 0), stop=False))
                b.op("tensor", lambda e, pdst=pdst, m=m: e.matmul(
                    pdst[:, 0:T2], cbrow_sb[0:1, m * 128:(m + 1) * 128], ONES_ROW_BF[0:1, 0:T2],
                    start=False, stop=True))
                vmm = b.mark("tensor")
                b.wait("vector", "PE", vmm)
                b.use("vector", f"ot{rb}")
                vo = b.op("vector", lambda e, pdst=pdst, rb=rb, m=m: e.tensor_add(
                    out=ot_sb[:, rb * T2:(rb + 1) * T2], in0=pdst[:, 0:T2],
                    in1=x14_sb[:, m * T2:(m + 1) * T2]), inc="DVE")
                b.rel(f"mm{rb}", "DVE", vo)
                b.wait("gpsimd", "DVE", vo)
                vst = b.dma("gpsimd", out_d[m * 128:(m + 1) * 128, t0:t0 + T2],
                            ot_sb[:, rb * T2:(rb + 1) * T2], "ST")
                b.rel(f"ot{rb}", "ST", vst)
            b.rel("hrall", "PE", b.cnt["PE"])
            b.rel("x1pall", "PE", b.cnt["PE"])
            b.rel("p3t", "PE", b.cnt["PE"])
            b.rel("p3t", "DVE", b.cnt["DVE"])

        b.wait("sync", "ST", b.cnt["ST"])
        b.flush()

    ctx.close()
    return nc


def host_inputs(inputs):
    f32 = np.float32
    WT = np.ascontiguousarray(np.asarray(inputs["L0_attn_W_in"], f32).T)
    cols = np.concatenate([np.arange(0, 4096), np.arange(6144, 8224), np.arange(8224, 8232)])
    winT = np.ascontiguousarray(WT[:, cols])
    cw = np.asarray(inputs["L0_attn_conv_w"], f32)[:, 0, :]
    cb = np.asarray(inputs["L0_attn_conv_b"], f32)
    shared = dict(
        winT=winT,
        wgT=np.ascontiguousarray(np.asarray(inputs["L0_attn_W_gate"], f32).T),
        woT=np.ascontiguousarray(np.asarray(inputs["L0_attn_W_out"], f32).T),
        wfcT=np.ascontiguousarray(np.asarray(inputs["L0_ffn_W_fc"], f32).T),
        wprT=np.ascontiguousarray(np.asarray(inputs["L0_ffn_W_proj"], f32).T),
        cwx=np.ascontiguousarray(cw[:2048].reshape(16, 128, 4).transpose(1, 0, 2).reshape(128, 64)),
        cwb=np.ascontiguousarray(cw[2048:2064]),
        cwc=np.ascontiguousarray(cw[2064:2080]),
        cbx=np.ascontiguousarray(cb[:2048].reshape(16, 128).T),
        cbb=np.ascontiguousarray(cb[2048:2064].reshape(16, 1)),
        cbc=np.ascontiguousarray(cb[2064:2080].reshape(16, 1)),
        dtb=np.ascontiguousarray(np.asarray(inputs["L0_attn_dt_bias"], f32).reshape(NH, 1)),
        alog=np.ascontiguousarray(np.asarray(inputs["L0_attn_A_log"], f32).reshape(NH, 1)),
        dpar=np.ascontiguousarray(np.asarray(inputs["L0_attn_D_param"], f32).reshape(NH, 1)),
        lng=np.ascontiguousarray(np.asarray(inputs["L0_attn_ln_gamma"], f32).reshape(16, 128).T),
        lnbt=np.ascontiguousarray(np.asarray(inputs["L0_attn_ln_beta"], f32).reshape(16, 128).T),
        bgat=np.ascontiguousarray(np.asarray(inputs["L0_attn_b_gate"], f32).reshape(16, 128).T),
        flnw=np.ascontiguousarray(np.asarray(inputs["L0_ffn_ln_w"], f32).reshape(32, 128).T),
        flnb=np.ascontiguousarray(np.asarray(inputs["L0_ffn_ln_b"], f32).reshape(32, 128).T),
    )
    consf = np.zeros((128, 768), f32)
    consf[:, 0:128] = np.eye(128, dtype=f32)
    consf[:, 128:656] = 1.0
    consbv = np.zeros((128, 768), np.float32)
    consbv[:, 0:128] = np.eye(128)
    consbv[:, 128:256] = np.triu(np.ones((128, 128)))
    consbv[:, 256:768] = 1.0
    shared["cons"] = consf
    shared["consb"] = consbv.astype(ml_dtypes.bfloat16)
    u = np.asarray(inputs["L0_attn_u"], f32)
    uTs = [np.ascontiguousarray(u[c].T) for c in range(8)]
    return shared, uTs


def kernel(**inputs):
    shared, uTs = host_inputs(inputs)
    nc = build_program(debug=False)
    in_maps = [dict(shared, uT=uTs[c]) for c in range(8)]
    res = run_bass_kernel_spmd(nc, in_maps, core_ids=list(range(8)))
    out = np.stack([np.asarray(r["out"]).T for r in res.results], axis=0)
    return out.astype(np.float32)
